# version 12
# baseline (speedup 1.0000x reference)
import numpy as np
import ml_dtypes
import concourse.bass as bass
import concourse.mybir as mybir
from concourse.bass_utils import run_bass_kernel_spmd

F32 = mybir.dt.float32
BF16 = mybir.dt.bfloat16
AF = mybir.ActivationFunctionType
ALU = mybir.AluOpType
AX = mybir.AxisListType

B, SQ, H = 2, 4096, 2048
HEADS, G, KVC = 16, 4, 128
FF = 5632
NQO = 1024
SCALE = float(128 ** -0.5)
MAGIC = float(np.float32(12582912.0))
LN_EPS = 1e-5
FFN_EPS = 1e-8
NW = float(FF * H)
PK = [32, 24, 16, 8]
SLOT_BLOCKS = [
    [15, 10, 5, 0],
    [14, 11, 4, 1],
    [13, 8, 7, 2],
    [12, 9, 6, 3],
]
NPT = SQ // 128
GCH = 1536


class Prog:
    ENGS = ("sync", "gpsimd", "vector", "scalar", "tensor")

    def __init__(self, nc):
        self.nc = nc
        self.stream = {e: [] for e in self.ENGS}
        self.cnt = {}
        self.sems = {}

    def sem(self, name):
        self.sems[name] = self.nc.alloc_semaphore(name)
        self.cnt[name] = 0

    def emit(self, eng, fn, waits=(), inc=None, inc_n=1):
        entry = [list(waits), fn, (inc, inc_n) if inc else None]
        self.stream[eng].append(entry)
        if inc:
            self.cnt[inc] += inc_n
            return self.cnt[inc]
        return None

    def reserve(self, eng, inc, inc_n):
        entry = [[], None, (inc, inc_n)]
        self.stream[eng].append(entry)
        self.cnt[inc] += inc_n
        return entry, self.cnt[inc]

    def mark(self, name):
        return self.cnt[name]


def build_nc():
    nc = bass.Bass("TRN2", target_bir_lowering=False)

    xbt = nc.dram_tensor("xbt", [H, SQ], F32, kind="ExternalInput").ap()
    xot = nc.dram_tensor("xot", [H, NQO], F32, kind="ExternalInput").ap()
    xo = nc.dram_tensor("xo", [NQO, H], F32, kind="ExternalInput").ap()
    wq = nc.dram_tensor("wq", [H, H], F32, kind="ExternalInput").ap()
    wkv = nc.dram_tensor("wkv", [H, 2 * G * KVC], F32, kind="ExternalInput").ap()
    wd = nc.dram_tensor("wd", [H, H], F32, kind="ExternalInput").ap()
    w1t = nc.dram_tensor("w1t", [H, FF], F32, kind="ExternalInput").ap()
    w2t = nc.dram_tensor("w2t", [H, FF], F32, kind="ExternalInput").ap()
    w3t = nc.dram_tensor("w3t", [FF, H], F32, kind="ExternalInput").ap()
    qposd = nc.dram_tensor("qposd", [128, NQO], F32, kind="ExternalInput").ap()
    kposd = nc.dram_tensor("kposd", [128, 32], F32, kind="ExternalInput").ap()
    identd = nc.dram_tensor("identd", [128, 128], BF16, kind="ExternalInput").ap()
    onesbd = nc.dram_tensor("onesbd", [128, 1], BF16, kind="ExternalInput").ap()
    onesfd = nc.dram_tensor("onesfd", [1, 128], F32, kind="ExternalInput").ap()
    magicd = nc.dram_tensor("magicd", [128, 1], F32, kind="ExternalInput").ap()
    out = nc.dram_tensor("out", [NQO, H], F32, kind="ExternalOutput").ap()
    gscr = nc.dram_tensor("gscr", [128, 3], F32, kind="Internal").ap()

    def sb(name, shape, dt=F32):
        return nc.alloc_sbuf_tensor(name, shape, dt).ap()

    Ay = sb("Ay", [128, 16384], F32)
    Ay_b = Ay.bitcast(BF16)
    kT = Ay_b[:, 0:16384]
    kT_r = kT.rearrange("p (g t) -> p g t", g=4)
    vsb = Ay_b[:, 16384:32768]
    Aq = sb("Aq", [128, 16384], BF16)
    Aw = sb("Aw", [128, 16384], BF16)
    Ac = sb("Ac", [128, 8192], BF16)
    cbuf = sb("cbuf", [128, 4224], BF16)
    cb32 = cbuf.bitcast(F32)
    qpos = cb32[:, 0:1024]
    binv = cb32[:, 1024:2048]
    kpos = cb32[:, 2048:2080]
    expb = [Ac[:, 1024:1536], Ac[:, 1536:2048], Ac[:, 2048:2560],
            Ac[:, 6656:7168], Ac[:, 7168:7680], Ac[:, 7680:8192]]
    expm = [Ac[:, 2560 + 512 * i: 3072 + 512 * i] for i in range(8)]
    hgr = cbuf[:, 0:4096]

    Ax = sb("Ax", [128, 16384], BF16)
    Ax32 = Ax.bitcast(F32)
    xst = [Ax32[:, 0:2048], Ax32[:, 2048:4096]]
    xTg = [Ax[:, 0:8192], Ax[:, 8192:16384]]
    gst = sb("gst", [128, 2 * GCH], F32)
    fst = [gst[:, 0:512], gst[:, 512:1024]]
    hst = [gst[:, 1024:2048], gst[:, 2048:3072]]
    xlnb = [cbuf[:, 0:2048], cbuf[:, 2048:4096]]
    Ac32 = Ac.bitcast(F32)
    mb = [Ac32[:, 0:512], Ac32[:, 512:1024]]
    rb = [Ac32[:, 1024:1536], Ac32[:, 1536:2048]]
    ctmp = [Ac32[:, 2048:2560], Ac32[:, 2560:3072]]
    csVb = Ac32[:, 3072:3584]
    x2b = [Ac[:, 7168:7680], Ac[:, 7680:8192]]
    rows1 = sb("rows1", [128, 1024], F32)
    rows2 = sb("rows2", [128, 1024], F32)
    mrow = rows1[0:1, 0:512]
    nrow = rows1[0:1, 512:1024]
    e2row = rows2[0:1, 0:512]
    m2row = rows2[0:1, 512:1024]
    sdrow = rows2[0:1, 0:512]
    mcol = [sb("mcol0", [128, 8], F32), sb("mcol1", [128, 8], F32)]
    csKc = sb("csKc", [128, 4], F32)
    csQc = sb("csQc", [128, 16], F32)
    sg_buf = [Ax[:, 0:1024], Ax[:, 1024:2048]]
    riv = rows1[0:1, 0:1024]
    bn24 = sb("bn24", [128, 24], F32)
    stats = sb("stats", [128, 4], F32)
    vtmp = sb("vtmp", [128, 8], F32)
    redcol = sb("redcol", [128, 224], F32)
    gsum = sb("gsum", [128, 4], F32)
    grow = rows2[0:1, 512:1024]
    g1 = sb("g1", [1, 16], F32)
    ginv = sb("ginv", [128, 4], F32)
    ident = sb("ident", [128, 128], BF16)
    onesb = sb("onesb", [128, 1], BF16)
    onesf = sb("onesf", [1, 128], F32)
    magicb = sb("magicb", [128, 1], F32)

    pb = [nc.alloc_psum_tensor(f"pb{i}", [128, 512], F32).ap() for i in range(8)]
    pbh = [p.bitcast(BF16) for p in pb]

    P = Prog(nc)
    for s in ("dc", "dx0", "dx1", "dkv", "dq0", "dq1", "dq2", "dq3",
              "dd0", "dd1", "dg0x", "dg1x", "dg0", "dg1", "df0", "df1", "df2", "df3",
              "dh0", "dh1", "dgb", "dob", "v", "s", "t", "g", "gv", "gsc"):
        P.sem(s)
    E = P.emit
    M = P.mark

    E("sync", lambda e: e.dma_start(out=qpos, in_=qposd), inc="dc", inc_n=16)
    E("sync", lambda e: e.dma_start(out=kpos, in_=kposd), inc="dc", inc_n=16)
    E("sync", lambda e: e.dma_start(out=ident, in_=identd), inc="dc", inc_n=16)
    E("sync", lambda e: e.dma_start(out=onesb, in_=onesbd), inc="dc", inc_n=16)
    E("sync", lambda e: e.dma_start(out=onesf, in_=onesfd), inc="dc", inc_n=16)
    E("sync", lambda e: e.dma_start(out=magicb, in_=magicd), inc="dc", inc_n=16)
    CONST_OK = M("dc")

    wkv_r = wkv.rearrange("(t p) e -> p t e", p=128)
    wq_r = wq.rearrange("(t p) e -> p t e", p=128)
    wd_r = wd.rearrange("(t p) e -> p t e", p=128)

    def cast_wq2(qq):
        def f(e, qq=qq):
            return e.dma_start(out=Aw.rearrange("p (t e) -> p t e", t=16),
                               in_=wq_r[:, :, qq * 512:(qq + 2) * 512])
        return f

    def cast_wd(h):
        def f(e, h=h):
            return e.dma_start(out=Aw.rearrange("p (t e) -> p t e", t=16),
                               in_=wd_r[:, :, h * 1024:(h + 1) * 1024])
        return f

    def cast_xg(tg, sl):
        def f(e, tg=tg, sl=sl):
            return e.dma_start(out=xTg[sl].rearrange("p (t m) -> p t m", t=16),
                               in_=xbt[:, tg * 512:(tg + 1) * 512]
                               .rearrange("(t p) m -> p t m", p=128))
        return f

    def cast_xo(grp, sl):
        def f(e, grp=grp, sl=sl):
            return e.dma_start(out=xTg[sl].rearrange("p (t m) -> p t m", t=16),
                               in_=xot[:, grp * 512:(grp + 1) * 512]
                               .rearrange("(t p) m -> p t m", p=128))
        return f

    E("gpsimd", lambda e: e.dma_start(out=Aw.rearrange("p (t e) -> p t e", t=16),
                                      in_=wkv_r), inc="dkv", inc_n=16)
    E("gpsimd", cast_xg(0, 0), inc="dq0", inc_n=16)
    E("gpsimd", cast_xg(1, 1), inc="dq1", inc_n=16)
    xg_sem = {0: ("dq0", 16), 1: ("dq1", 16)}

    gchunks = []
    for mi, wt in enumerate((w1t, w2t)):
        for t in range(16):
            for c0 in range(0, FF, GCH):
                cw = min(GCH, FF - c0)
                gchunks.append((mi, wt[t * 128:(t + 1) * 128, c0:c0 + cw], cw))
    for t in range(44):
        for c0 in (0, 1024):
            gchunks.append((2, w3t[t * 128:(t + 1) * 128, c0:c0 + 1024], 1024))
    mat_cols = {0: [0, 63], 1: [64, 127], 2: [128, 215]}

    gpend = []
    for ci, (mi, src_, cw) in enumerate(gchunks):
        sl = ci % 2
        dsem = "dg0" if sl == 0 else "dg1"
        psem = "gv" if sl == 0 else "gsc"
        waits = [(dsem, (ci // 2) * 16)] if ci >= 2 else []
        if ci >= 2:
            waits.append((psem, ci // 2))

        def fd(e, src_=src_, cw=cw, sl=sl):
            return e.dma_start(out=gst[:, sl * GCH: sl * GCH + cw], in_=src_)
        gm = E("sync", fd, waits=waits, inc=dsem, inc_n=16)

        if sl == 0:
            def fr(e, cw=cw, sl=sl, ci=ci):
                return e.reduce_sum(out=redcol[:, ci:ci + 1],
                                    in_=gst[:, sl * GCH: sl * GCH + cw],
                                    axis=AX.X, apply_absolute_value=True)
            gpend.append((ci, "vector", fr, (dsem, gm), psem))
        else:
            def fr(e, cw=cw, sl=sl, ci=ci):
                return e.activation(out=gst[:, sl * GCH: sl * GCH + cw],
                                    in_=gst[:, sl * GCH: sl * GCH + cw],
                                    func=AF.Abs,
                                    accum_out=redcol[:, ci:ci + 1])
            gpend.append((ci, "scalar", fr, (dsem, gm), psem))

    def flush_gamma(k, limit):
        m = 0
        while gpend and m < k and gpend[0][0] <= limit:
            ci_, eng, fr, wt, psem = gpend.pop(0)
            E(eng, fr, waits=[wt], inc=psem)
            m += 1

    GAMMA_BOUNCE = [None, None, None]


    mscr = nc.dram_tensor("mscr", [2, 2, 512], F32, kind="Internal").ap()

    B_SQ_SET = "exp_and_others"

    w = [("dkv", 16), ("dc", CONST_OK)]
    for g4 in range(4):
        for t in range(16):
            def fcs(e, g4=g4, t=t):
                return e.matmul(out=pb[6][:, g4:g4 + 1],
                                lhsT=Aw[:, t * 1024 + g4 * 128: t * 1024 + (g4 + 1) * 128],
                                rhs=onesb, start=(t == 0), stop=(t == 15))
            csm = E("tensor", fcs, waits=(w if g4 == 0 and t == 0 else ()),
                    inc="t" if t == 15 else None)
    def fcse(e):
        return e.copy(out=csKc, in_=pb[6][:, 0:4])
    cske = E("scalar", fcse, waits=[("t", csm)], inc="s")
    for t in range(16):
        def fcv(e, t=t):
            return e.matmul(out=pb[7][0:1, 0:512], lhsT=onesb,
                            rhs=Aw[:, t * 1024 + 512: (t + 1) * 1024],
                            start=(t == 0), stop=(t == 15))
        cvm = E("tensor", fcv, inc="t" if t == 15 else None)
    def fcve(e):
        return e.copy(out=sdrow, in_=pb[7][0:1, 0:512])
    cve = E("scalar", fcve, waits=[("t", cvm)], inc="s")
    def fcvb(e):
        return e.matmul(out=pb[7][:, 0:512], lhsT=onesf, rhs=sdrow,
                        start=True, stop=True)
    cvbm = E("tensor", fcvb, waits=[("s", cve)], inc="t")
    def fcvbe(e):
        return e.copy(out=csVb, in_=pb[7][:, 0:512])
    cvbe = E("scalar", fcvbe, waits=[("t", cvbm)], inc="s")

    bank_free = {}
    stats_marks = {}
    kv_done = {}
    corr_done = {}
    x2_marks = {}
    stats_state = {"colm": None, "ce": None}

    def emit_stats_sums(tg, after_w=()):
        sl = tg % 2
        wsq = list(after_w)
        if stats_state["ce"] is not None:
            wsq.append(("s", stats_state["ce"]))
        smm = None
        for t in range(16):
            def fsm(e, t=t, sl=sl):
                return e.matmul(out=pb[6][0:1, 0:512], lhsT=onesb,
                                rhs=xTg[sl][:, t * 512:(t + 1) * 512],
                                start=(t == 0), stop=(t == 15))
            smm = E("tensor", fsm, waits=(wsq if t == 0 else ()),
                    inc="t" if t == 15 else None)
        stats_state["smm"] = smm
        stats_state["sums_w"] = list(after_w)

    def emit_stats_sq(tg):
        sl = tg % 2
        after_w = stats_state["sums_w"]
        sqm = None
        for t in range(16):
            def fx2(e, t=t, sl=sl):
                return e.activation(out=x2b[t % 2],
                                    in_=xTg[sl][:, t * 512:(t + 1) * 512],
                                    func=AF.Square)
            wq2 = list(after_w) if t == 0 else []
            if t == 1 and stats_state["ce"] is not None:
                pass
            if t >= 2:
                wq2.append(("t", x2_marks[(tg, t - 2)]))
            s2 = E("scalar", fx2, waits=wq2, inc="s")

            def fsq(e, t=t):
                return e.matmul(out=pb[7][0:1, 0:512], lhsT=onesb, rhs=x2b[t % 2],
                                start=(t == 0), stop=(t == 15))
            wq3 = [("s", s2)]
            if t == 0 and stats_state["ce"] is not None:
                wq3.append(("s", stats_state["ce"]))
            sqm = E("tensor", fsq, waits=wq3, inc="t")
            x2_marks[(tg, t)] = sqm
        stats_state["sqm"] = sqm

    def emit_stats_rows(tg, mbuser=None):
        sl = tg % 2
        smm, sqm = stats_state["smm"], stats_state["sqm"]
        wfm = [("t", smm), ("t", sqm)]
        if stats_state["colm"] is not None:
            wfm.append(("t", stats_state["colm"]))

        def f_m(e):
            return e.tensor_scalar(out=mrow, in0=pb[6][0:1, 0:512],
                                   scalar1=1.0 / 2048.0, scalar2=None, op0=ALU.mult)
        vm1 = E("vector", f_m, waits=wfm, inc="v")

        def f_e2(e):
            return e.tensor_scalar(out=e2row, in0=pb[7][0:1, 0:512],
                                   scalar1=1.0 / 2048.0, scalar2=None, op0=ALU.mult)
        vm2 = E("vector", f_e2, waits=[("v", vm1)], inc="v")

        def f_m2(e):
            return e.tensor_tensor(out=m2row, in0=mrow, in1=mrow, op=ALU.mult)
        vm3 = E("vector", f_m2, waits=[("v", vm2)], inc="v")

        def f_ve(e):
            return e.scalar_tensor_tensor(out=m2row, in0=e2row, scalar=LN_EPS,
                                          in1=m2row, op0=ALU.add, op1=ALU.subtract)
        vm4 = E("vector", f_ve, waits=[("v", vm3)], inc="v")

        def f_sd(e):
            return e.activation(out=sdrow, in_=m2row, func=AF.Sqrt)
        sm1 = E("scalar", f_sd, waits=[("v", vm4)], inc="s")

        def f_rc(e):
            return e.reciprocal(out=nrow, in_=sdrow)
        vm5 = E("vector", f_rc, waits=[("s", sm1)], inc="v")

        def f_ng(e):
            return e.tensor_scalar(out=nrow, in0=nrow, scalar1=-1.0,
                                   scalar2=None, op0=ALU.mult)
        vm6 = E("vector", f_ng, waits=[("v", vm5)], inc="v")

        wbm = [("v", vm6)]
        if mbuser is not None:
            wbm.append(("v", mbuser))

        def f_bm(e):
            return e.matmul(out=pb[6][:, 0:512], lhsT=onesf, rhs=mrow,
                            start=True, stop=True)
        bm = E("tensor", f_bm, waits=wbm, inc="t")

        def f_bn(e):
            return e.matmul(out=pb[7][:, 0:512], lhsT=onesf, rhs=nrow,
                            start=True, stop=True)
        bn_ = E("tensor", f_bn, inc="t")

        def f_em(e, sl=sl):
            return e.copy(out=mb[sl], in_=pb[6][:, 0:512])
        em1 = E("scalar", f_em, waits=[("t", bm)], inc="s")

        def f_en(e, sl=sl):
            return e.copy(out=rb[sl], in_=pb[7][:, 0:512])
        em2 = E("scalar", f_en, waits=[("t", bn_)], inc="s")
        colm = None
        for tt in range(4):
            def f_cm(e, tt=tt):
                return e.matmul(out=pb[6][:, 8 + tt:9 + tt],
                                lhsT=mrow[0:1, tt * 128:(tt + 1) * 128],
                                rhs=onesf[0:1, 0:1], start=True, stop=True)
            colm = E("tensor", f_cm, waits=([("s", em1)] if tt == 0 else ()),
                     inc="t" if tt == 3 else None)

            def f_cn(e, tt=tt):
                return e.matmul(out=pb[6][:, 12 + tt:13 + tt],
                                lhsT=nrow[0:1, tt * 128:(tt + 1) * 128],
                                rhs=onesf[0:1, 0:1], start=True, stop=True)
            colm = E("tensor", f_cn, inc="t" if tt == 3 else None)

        def f_ce(e, sl=sl):
            return e.copy(out=mcol[sl], in_=pb[6][:, 8:16])
        ce = E("scalar", f_ce, waits=[("t", colm)], inc="s")
        stats_state["colm"] = colm
        stats_state["ce"] = ce
        return dict(bcast=(em1, em2), col=ce, vm=vm6)

    def emit_stats(tg, after_w=(), mbuser=None):
        emit_stats_sums(tg, after_w)
        emit_stats_sq(tg)
        return emit_stats_rows(tg, mbuser)

    prev_corr = {}
    for tg in range(8):
        sl = tg % 2
        if tg == 0:
            stats_marks[0] = emit_stats(0, after_w=[xg_sem[0], ("s", cvbe)],
                                        mbuser=None)
        st = stats_marks[tg]
        if tg + 1 <= 7:
            emit_stats_sums(tg + 1, after_w=[xg_sem[tg + 1]])
        kw = [("s", st["bcast"][1]), ("s", st["col"])]
        km = None
        for g4 in range(4):
            kwg = list(kw) if g4 == 0 else []
            if tg >= 1:
                kwg.append(("v", bank_free[(tg - 1, g4)]))
            for t in range(16):
                def fk(e, g4=g4, t=t, sl=sl):
                    return e.matmul(out=pb[g4],
                                    lhsT=Aw[:, t * 1024 + g4 * 128: t * 1024 + (g4 + 1) * 128],
                                    rhs=xTg[sl][:, t * 512:(t + 1) * 512],
                                    start=(t == 0), stop=(t == 15))
                km = E("tensor", fk, waits=(kwg if t == 0 else ()),
                       inc="t" if t == 15 else None)
            def fc1(e, g4=g4, sl=sl):
                return e.scalar_tensor_tensor(out=ctmp[g4 % 2], in0=mb[sl],
                                              scalar=csKc[:, g4:g4 + 1],
                                              in1=pb[g4], op0=ALU.mult,
                                              op1=ALU.subtract)
            wc = [("t", km)]
            if g4 >= 2:
                wc.append(("v", prev_corr[(tg, g4 - 2)]))
            c1 = E("vector", fc1, waits=wc, inc="v")
            bank_free[(tg, g4)] = c1
            def fc2(e, g4=g4, sl=sl, tg=tg):
                return e.tensor_tensor(out=kT[:, g4 * 4096 + tg * 512:
                                               g4 * 4096 + (tg + 1) * 512],
                                       in0=ctmp[g4 % 2], in1=rb[sl], op=ALU.mult)
            c2 = E("vector", fc2, waits=[("v", c1)], inc="v")
            prev_corr[(tg, g4)] = c2
            flush_gamma(2, [17, 33, 49, 65, 81, 97, 113, 115][tg])
        if tg + 1 <= 7:
            emit_stats_sq(tg + 1)
        vlast = None
        for tt in range(4):
            vb = 4 + tt % 2
            vw = []
            if tt >= 2:
                vw.append(("v", bank_free[("v", tg, tt - 2)]))
            elif tg >= 1:
                vw.append(("v", bank_free[("v", tg - 1, tt + 2)]))
            for t in range(16):
                def fv(e, tt=tt, t=t, sl=sl, vb=vb):
                    return e.matmul(out=pb[vb],
                                    lhsT=xTg[sl][:, t * 512 + tt * 128: t * 512 + (tt + 1) * 128],
                                    rhs=Aw[:, t * 1024 + 512: t * 1024 + 1024],
                                    start=(t == 0), stop=(t == 15))
                vm_ = E("tensor", fv, waits=(vw if t == 0 else ()),
                        inc="t" if t == 15 else None)
            def fvc1(e, tt=tt, sl=sl, vb=vb):
                return e.scalar_tensor_tensor(out=ctmp[tt % 2], in0=csVb,
                                              scalar=mcol[sl][:, tt:tt + 1],
                                              in1=pb[vb], op0=ALU.mult,
                                              op1=ALU.subtract)
            v1 = E("vector", fvc1,
                   waits=[("t", vm_)] + ([("v", prev_corr[("v", tg, tt - 2)])]
                                         if tt >= 2 else []), inc="v")
            bank_free[("v", tg, tt)] = v1
            def fvc2(e, tt=tt, sl=sl, tg=tg):
                return e.tensor_scalar(out=vsb[:, (tg * 4 + tt) * 512:
                                                (tg * 4 + tt + 1) * 512],
                                       in0=ctmp[tt % 2],
                                       scalar1=mcol[sl][:, 4 + tt:5 + tt],
                                       scalar2=None, op0=ALU.mult)
            v2 = E("vector", fvc2, waits=[("v", v1)], inc="v")
            prev_corr[("v", tg, tt)] = v2
            vlast = v2
            flush_gamma(2, [17, 33, 49, 65, 81, 97, 113, 115][tg])
        kv_done[tg] = M("t")
        corr_done[tg] = vlast
        if tg + 1 <= 7:
            stats_marks[tg + 1] = emit_stats_rows(tg + 1,
                                                  mbuser=corr_done.get(tg - 1))

        if tg + 2 <= 7:
            sem = "dq0" if sl == 0 else "dq1"
            mk = E("gpsimd", cast_xg(tg + 2, sl), waits=[("v", vlast)],
                   inc=sem, inc_n=16)
            xg_sem[tg + 2] = (sem, mk)
        elif tg + 2 <= 9:
            sem = "dq2" if tg + 2 == 8 else "dq3"
            mk = E("gpsimd", cast_xo(tg + 2 - 8, sl), waits=[("v", vlast)],
                   inc=sem, inc_n=16)
            xg_sem[tg + 2] = (sem, mk)

    B_KV_T = M("t")
    E("gpsimd", cast_wq2(0), waits=[("t", B_KV_T)], inc="dd0", inc_n=16)
    WQ01 = ("dd0", 16)

    def emit_csq(h0):
        cwq = [WQ01 if h0 == 0 else WQ23]
        if stats_state["ce"] is not None:
            cwq.append(("s", stats_state["ce"]))
        cm = None
        for hh in range(8):
            h = h0 + hh
            aw_off = ((h // 4) % 2) * 512 + (h % 4) * 128
            for t in range(16):
                def fcq(e, t=t, aw_off=aw_off, hh=hh):
                    return e.matmul(out=pb[6][:, hh:hh + 1],
                                    lhsT=Aw[:, t * 1024 + aw_off:
                                            t * 1024 + aw_off + 128],
                                    rhs=onesb, start=(t == 0), stop=(t == 15))
                cm = E("tensor", fcq, waits=(cwq if hh == 0 and t == 0 else ()),
                       inc="t" if t == 15 else None)

        def fcqe(e, h0=h0):
            return e.copy(out=csQc[:, h0:h0 + 8], in_=pb[6][:, 0:8])
        return E("scalar", fcqe, waits=[("t", cm)], inc="s")

    csq_mark = {0: emit_csq(0), 8: None}
    q_marks = {}
    st8 = emit_stats(8, after_w=[xg_sem[8], ("s", csq_mark[0])],
                     mbuser=corr_done.get(6))
    st9 = emit_stats(9, after_w=[xg_sem[9]], mbuser=corr_done.get(7))
    sts = {0: st8, 1: st9}
    for h in range(16):
        flush_gamma(3 if h < 8 else 1, 135 if h < 8 else 194)
        if h == 8:
            csq_mark[8] = emit_csq(8)
        qq = h // 4
        aw_off = (qq % 2) * 512 + (h % 4) * 128
        for grp in range(2):
            sl = grp
            st = sts[grp]
            qb = grp
            wqm = [("s", st["bcast"][1]), ("s", st["col"])]
            wqm.append(WQ01 if qq < 2 else WQ23)
            if h >= 8:
                wqm.append(("s", csq_mark[8]))
            if h == 0 and grp == 0:
                wqm.append(("v", corr_done[7]))
            prev2 = bank_free.get(("q", h - 1, grp))
            if prev2 is not None:
                wqm.append(("v", prev2))
            qm_ = None
            for t in range(16):
                def fq_(e, t=t, sl=sl, aw_off=aw_off, qb=qb):
                    return e.matmul(out=pb[qb],
                                    lhsT=Aw[:, t * 1024 + aw_off:
                                            t * 1024 + aw_off + 128],
                                    rhs=xTg[sl][:, t * 512:(t + 1) * 512],
                                    start=(t == 0), stop=(t == 15))
                qm_ = E("tensor", fq_, waits=(wqm if t == 0 else ()),
                        inc="t" if t == 15 else None)

            def fqc1(e, h=h, sl=sl, qb=qb):
                return e.scalar_tensor_tensor(out=ctmp[qb], in0=mb[sl],
                                              scalar=csQc[:, h:h + 1],
                                              in1=pb[qb], op0=ALU.mult,
                                              op1=ALU.subtract)
            wq1 = [("t", qm_)]
            if q_marks.get((h - 1, grp)) is not None:
                wq1.append(("v", q_marks[(h - 1, grp)]))
            q1 = E("vector", fqc1, waits=wq1, inc="v")
            bank_free[("q", h, grp)] = q1

            def fqc2(e, h=h, sl=sl, grp=grp, qb=qb):
                return e.tensor_tensor(out=Aq[:, h * 1024 + grp * 512:
                                              h * 1024 + (grp + 1) * 512],
                                       in0=ctmp[qb], in1=rb[sl], op=ALU.mult)
            q2 = E("vector", fqc2, waits=[("v", q1)], inc="v")
            q_marks[(h, grp)] = q2
            if h == 7 and grp == 1:
                E("gpsimd", cast_wq2(2), waits=[("v", q2)], inc="dd1", inc_n=16)
                WQ23 = ("dd1", 16)

    B2_PE_DONE = M("t")
    B2_S_DONE = M("s")
    B2_V_DONE = M("v")

    def ln_chain(src_ap, sl, xdep):
        m4 = None
        for i in range(4):
            def f(e, i=i, src_ap=src_ap):
                return e.bn_stats(out=bn24[:, i * 6:(i + 1) * 6],
                                  in_=src_ap[:, i * 512:(i + 1) * 512])
            m4 = E("vector", f, waits=(xdep if i == 0 else ()), inc="v")

        def fa(e, sl=sl):
            return e.bn_aggr(out=stats[:, sl * 2:sl * 2 + 2], in_=bn24)
        ma = E("vector", fa, waits=[("v", m4)], inc="v")

        def fe(e, sl=sl):
            return e.tensor_scalar(out=vtmp[:, sl * 4:sl * 4 + 1],
                                   in0=stats[:, sl * 2 + 1:sl * 2 + 2],
                                   scalar1=LN_EPS, scalar2=None, op0=ALU.add)
        vm = E("vector", fe, waits=[("v", ma)], inc="v")

        def fs(e, sl=sl):
            return e.activation(out=vtmp[:, sl * 4 + 1:sl * 4 + 2],
                                in_=vtmp[:, sl * 4:sl * 4 + 1], func=AF.Sqrt)
        sm = E("scalar", fs, waits=[("v", vm)], inc="s")

        def fq(e, sl=sl):
            return e.reciprocal(out=vtmp[:, sl * 4 + 2:sl * 4 + 3],
                                in_=vtmp[:, sl * 4 + 1:sl * 4 + 2])
        mq = E("vector", fq, waits=[("s", sm)], inc="v")

        def fn_(e, sl=sl):
            return e.tensor_scalar(out=vtmp[:, sl * 4 + 3:sl * 4 + 4],
                                   in0=stats[:, sl * 2:sl * 2 + 1],
                                   scalar1=vtmp[:, sl * 4 + 2:sl * 4 + 3],
                                   scalar2=-1.0, op0=ALU.mult, op1=ALU.mult)
        vm2 = E("vector", fn_, waits=[("v", mq)], inc="v")
        return vm2, vtmp[:, sl * 4 + 2:sl * 4 + 3], vtmp[:, sl * 4 + 3:sl * 4 + 4]

    def transpose16(src, am, evac_fn, prev_evac=None):
        ev_marks = []
        last_t = None
        for tq in range(4):
            for i in range(4):
                tw = []
                if tq == 0 and i == 0:
                    tw = [("s", am), ("dc", CONST_OK)]
                if tq >= 2 and i == 0:
                    tw.append(("s", ev_marks[tq - 2]))
                if tq < 2 and i == 0 and prev_evac is not None:
                    tw.append(("s", prev_evac[tq - 2]))

                def ft(e, src=src, tq=tq, i=i):
                    return e.transpose(out=pbh[tq % 2][:, i * 128:(i + 1) * 128],
                                       in_=src[:, (tq * 4 + i) * 128:(tq * 4 + i + 1) * 128],
                                       identity=ident)
                last_t = E("tensor", ft, waits=tw, inc="t" if i == 3 else None)
            em = E("scalar", evac_fn(tq), waits=[("t", last_t)], inc="s")
            ev_marks.append(em)
        return last_t, ev_marks

    def ctx_dst(h):
        if h == 0:
            return Ac[:, 0:1024]
        return Aq[:, (h - 1) * 1024: h * 1024]

    ps_s = [pb[0], pb[1], pb[7]]

    def kt_units(kt):
        if kt < 16:
            return [(0, 512), (512, 512)]
        return [(0, 512)]

    def acc_parts(kt):
        if kt < 15:
            return [(0, 512, False), (512, 512, kt == 31)]
        if kt == 15:
            return [(0, 512, False), (512, 512, True)]
        return [(0, 512, kt == 31)]

    units = []
    for h in range(16):
        for kt in range(32):
            for (m0, w) in kt_units(kt):
                units.append((h, kt, m0, w))
    n_u = len(units)
    exp_marks = [None] * n_u
    stt_marks = [None] * n_u
    sum_done = [None] * n_u
    ctxbf_marks = {}
    recip_marks = {}
    binv_marks = {}
    bcast_marks = {}
    pend = {"tensor": [], "scalar": [], "vector": []}

    def flush(eng, upto):
        q = pend[eng]
        while q and q[0][0] <= upto:
            _, fn, wfn, inc, store, key = q.pop(0)
            m0 = E(eng, fn, waits=wfn(), inc=inc)
            if store is not None:
                store[key] = m0

    first_c = [("s", B2_S_DONE), ("t", B2_PE_DONE), ("v", B2_V_DONE),
               ("dc", CONST_OK)]

    def emit_ctx_sum(u):
        h, kt, m0, w = units[u]
        g = h // 4
        esl = u % 8
        wctx = [stt_marks[u]]
        if kt == 0 and h >= 1:
            wctx.append(("v", ctxbf_marks[h - 1]))
        wsum = []
        if kt == 0 and h >= 1:
            wsum.append(("v", recip_marks[h - 1]))
        parts = [(pm0, pw, stp) for (pm0, pw, stp) in acc_parts(kt)
                 if pm0 >= m0 and pm0 < m0 + w]
        last_t = None
        for (pm0, pw, stp) in parts:
            bank = 2 if pm0 < 512 else 3
            boff = pm0 if pm0 < 512 else pm0 - 512

            def fC(e, g=g, kt=kt, pm0=pm0, pw=pw, stp=stp, m0=m0, esl=esl,
                   bank=bank, boff=boff):
                return e.matmul(out=pb[bank][:, boff:boff + pw],
                                lhsT=vsb[:, kt * 512 + g * 128: kt * 512 + (g + 1) * 128],
                                rhs=expm[esl][:, pm0 - m0: pm0 - m0 + pw],
                                start=(kt == 0), stop=stp)
            E("tensor", fC, waits=wctx)
            wctx = []
            sbank = 4 if pm0 < 512 else 5

            def fSum(e, kt=kt, pm0=pm0, pw=pw, stp=stp, m0=m0, esl=esl,
                     sbank=sbank, boff=boff):
                return e.matmul(out=pb[sbank][0:1, boff:boff + pw],
                                lhsT=onesb, rhs=expm[esl][:, pm0 - m0: pm0 - m0 + pw],
                                start=(kt == 0), stop=stp)
            last_t = E("tensor", fSum, waits=wsum, inc="t")
            wsum = []
        sum_done[u] = last_t

    ctxq = []

    def drain_ctxq(u_now):
        while ctxq:
            u2 = ctxq[0]
            if u2 >= u_now:
                break
            h2, kt2, m02, _ = units[u2]
            if kt2 == 0 and m02 == 0 and h2 >= 1 and (h2 - 1) not in ctxbf_marks:
                break
            ctxq.pop(0)
            emit_ctx_sum(u2)

    for u, (h, kt, m0, w) in enumerate(units):
        g = h // 4
        psl = u % 3
        esl = u % 8

        flush("tensor", u)
        flush("scalar", u)
        flush("vector", u)

        wS = list(first_c) if u == 0 else []
        if u >= 3:
            wS.append(("s", exp_marks[u - 3]))

        def fS(e, g=g, kt=kt, h=h, m0=m0, w=w, psl=psl):
            return e.matmul(out=ps_s[psl][:, 0:w],
                            lhsT=kT[:, g * 4096 + kt * 128: g * 4096 + (kt + 1) * 128],
                            rhs=Aq[:, h * 1024 + m0: h * 1024 + m0 + w],
                            start=True, stop=True)
        sm_t = E("tensor", fS, waits=wS, inc="t")

        ctxq.append(u)
        drain_ctxq(u - 1)

        wE = [("t", sm_t)]
        if u >= 6:
            wE.append(stt_marks[u - 6])

        def fE(e, psl=psl, ebl=u % 6, w=w):
            return e.activation(out=expb[ebl][:, 0:w], in_=ps_s[psl][:, 0:w],
                                func=AF.Exp, scale=SCALE)
        em = E("scalar", fE, waits=wE, inc="s")
        exp_marks[u] = em

        if kt in (8, 16, 24) and m0 == 0:
            flush_gamma(2, 212)
        wM = [("s", em)]
        if u >= 8:
            wM.append(("t", sum_done[u - 8]))

        def fM_(e, m0=m0, w=w, kt=kt, esl=esl, ebl=u % 6):
            return e.scalar_tensor_tensor(out=expm[esl][:, 0:w],
                                          in0=qpos[:, m0:m0 + w],
                                          scalar=kpos[:, kt:kt + 1],
                                          in1=expb[ebl][:, 0:w],
                                          op0=ALU.is_ge, op1=ALU.mult)
        vm = ("v", E("vector", fM_, waits=wM, inc="v"))
        stt_marks[u] = vm

        if kt == 31:
            hh_ = h

            def fR(e):
                return e.reciprocal(out=riv, in_=pb[4].rearrange("p f -> p f")[0:1, 0:512])
            def fR0(e):
                return e.reciprocal(out=riv[:, 0:512], in_=pb[4][0:1, 0:512])
            fR0._x = None

            def fR1(e):
                return e.reciprocal(out=riv[:, 512:1024], in_=pb[5][0:1, 0:512])

            def wfR(u=u):
                return [("t", sum_done[u])]
            pend["vector"].append((u + 3, fR0, wfR, "v", None, None))
            pend["vector"].append((u + 3, fR1, wfR, "v", recip_marks, hh_))

            for half in range(2):
                def fB(e, half=half):
                    return e.matmul(out=pb[6][:, 0:512], lhsT=onesf,
                                    rhs=riv[:, half * 512:(half + 1) * 512],
                                    start=True, stop=True)

                def wfB(hh_=hh_, half=half):
                    w2 = [("v", recip_marks[hh_])]
                    if half == 0 and (hh_ - 1, 1) in binv_marks:
                        w2.append(("s", binv_marks[(hh_ - 1, 1)]))
                    if half == 1:
                        w2.append(("s", binv_marks[(hh_, 0)]))
                    return w2
                pend["tensor"].append((u + 4 + half * 2, fB, wfB, "t",
                                       bcast_marks, (hh_, half)))

                def fBe(e, half=half):
                    return e.copy(out=binv[:, half * 512:(half + 1) * 512],
                                  in_=pb[6][:, 0:512])

                def wfBe(hh_=hh_, half=half):
                    return [("t", bcast_marks[(hh_, half)])]
                pend["scalar"].append((u + 5 + half * 2, fBe, wfBe, "s",
                                       binv_marks, (hh_, half)))

            def fCB0(e, hh_=hh_):
                return e.tensor_tensor(out=ctx_dst(hh_)[:, 0:512],
                                       in0=pb[2][:, 0:512],
                                       in1=binv[:, 0:512], op=ALU.mult)

            def wfCB0(hh_=hh_):
                return [("s", binv_marks[(hh_, 0)])]
            pend["vector"].append((u + 6, fCB0, wfCB0, "v", None, None))

            def fCB1(e, hh_=hh_):
                return e.tensor_tensor(out=ctx_dst(hh_)[:, 512:1024],
                                       in0=pb[3][:, 0:512],
                                       in1=binv[:, 512:1024], op=ALU.mult)

            def wfCB1(hh_=hh_):
                return [("s", binv_marks[(hh_, 1)])]
            pend["vector"].append((u + 8, fCB1, wfCB1, "v", ctxbf_marks, hh_))
            flush_gamma(5, 212)

    for d in range(n_u, n_u + 12):
        flush("tensor", d)
        flush("scalar", d)
        flush("vector", d)
        drain_ctxq(d + 1)
    C_V = M("v")
    C_S = M("s")
    E("gpsimd", cast_wd(0), waits=[("t", B2_PE_DONE)], inc="dg0x", inc_n=16)
    flush_gamma(10000, 10000)
    for mi in range(3):
        c0, c1 = min(mat_cols[mi]), max(mat_cols[mi]) + 1

        def frm(e, mi=mi, c0=c0, c1=c1):
            return e.reduce_sum(out=gsum[:, mi:mi + 1], in_=redcol[:, c0:c1],
                                axis=AX.X)
        E("vector", frm, waits=[("gsc", M("gsc")), ("gv", M("gv"))], inc="gv")
    E("gpsimd", lambda e: e.dma_start(out=gscr, in_=gsum[:, 0:3]),
      waits=[("gv", M("gv"))], inc="dgb", inc_n=16)
    E("gpsimd", lambda e: e.dma_start(
        out=grow[:, 0:384], in_=gscr.rearrange("p m -> (p m)").unsqueeze(0)),
      waits=[("dgb", 16)], inc="dgb", inc_n=16)
    GB_OK = M("dgb")


    yadd_marks = {}
    d_last_mm = None
    for hh in range(2):
        for mt in range(8):
            didx = hh * 8 + mt
            sl = didx % 2
            dsem = "dx0" if sl == 0 else "dx1"
            w = ([("s", B2_S_DONE), ("t", B2_PE_DONE), ("v", B2_V_DONE)]
                 if didx < 2 else [("v", yadd_marks[didx - 2])])
            w.append((dsem, M(dsem)))

            def fd(e, mt=mt, hh=hh, sl=sl):
                return e.dma_start(out=xst[sl][:, 0:1024],
                                   in_=xo[mt * 128:(mt + 1) * 128,
                                          hh * 1024:(hh + 1) * 1024])
            dm = E("sync", fd, waits=w, inc=dsem, inc_n=16)

            pbase = 0 if sl == 0 else 2
            w = [("dg0x" if hh == 0 else "dg1x", 16), ("v", C_V), ("s", C_S)]
            if didx >= 2:
                w.append(("v", yadd_marks[didx - 2]))
            mmk = None
            for et in range(16):
                lsrc = ctx_dst(et)
                for nch in range(2):
                    def fm(e, mt=mt, et=et, nch=nch, pbase=pbase, lsrc=lsrc):
                        return e.matmul(
                            out=pb[pbase + nch],
                            lhsT=lsrc[:, mt * 128:(mt + 1) * 128],
                            rhs=Aw[:, et * 1024 + nch * 512: et * 1024 + (nch + 1) * 512],
                            start=(et == 0), stop=(et == 15))
                    mmk = E("tensor", fm, waits=(w if et == 0 and nch == 0 else ()),
                            inc="t" if (et == 15 and nch == 1) else None)
            ym = None
            for nch in range(2):
                def fy(e, mt=mt, hh=hh, nch=nch, sl=sl, pbase=pbase):
                    return e.tensor_tensor(
                        out=Ay[:, mt * 2048 + hh * 1024 + nch * 512:
                               mt * 2048 + hh * 1024 + (nch + 1) * 512],
                        in0=xst[sl][:, nch * 512:(nch + 1) * 512],
                        in1=pb[pbase + nch], op=ALU.add)
                ym = E("vector", fy,
                       waits=([("t", mmk), (dsem, dm)] if nch == 0 else ()), inc="v")
            yadd_marks[didx] = ym
            flush_gamma(6, 212 if hh == 0 else 10000)
        if hh == 0:
            d_last_mm = M("t")
            E("gpsimd", cast_wd(1), waits=[("t", d_last_mm)], inc="dg1x", inc_n=16)
    D_T = M("t")
    D_V = M("v")

    ylt_prev_evac = None
    for mt in range(8):
        sl = mt % 2
        xdep = [("v", D_V)] if mt == 0 else [("v", yadd_marks[8 + mt])]
        vm, rstd, nmr = ln_chain(Ay[:, mt * 2048:(mt + 1) * 2048], sl, xdep)

        def fap(e, mt=mt, sl=sl, rstd=rstd, nmr=nmr):
            return e.activation(out=xlnb[sl], in_=Ay[:, mt * 2048:(mt + 1) * 2048],
                                func=AF.Identity, bias=nmr, scale=rstd)
        am = E("scalar", fap, waits=[("v", vm)], inc="s")

        def mk_evac(tq):
            def fev(e, tq=tq, mt=mt):
                return e.copy(
                    out=Aq.rearrange("p (t m) -> p t m", t=16)[:, tq * 4:(tq + 1) * 4,
                                                               mt * 128:(mt + 1) * 128],
                    in_=pbh[tq % 2].rearrange("p (t m) -> p t m", t=8)[:, 0:4, :])
            return fev
        _, ev_marks = transpose16(xlnb[sl], am, mk_evac, ylt_prev_evac)
        ylt_prev_evac = ev_marks

    D2_T = M("t")
    D2_S = M("s")
    D2_V = M("v")

    ginv_last = None
    for mi in range(3):
        def f1(e, mi=mi):
            return e.reduce_sum(out=g1[:, mi:mi + 1],
                                in_=grow[:, 0:384].rearrange("o (p m) -> o m p", m=3)[:, mi, :],
                                axis=AX.X)
        mf1 = E("vector", f1, waits=[("dgb", GB_OK)], inc="v")

        def f2(e, mi=mi):
            return e.tensor_scalar(out=g1[:, 4 + mi:5 + mi], in0=g1[:, mi:mi + 1],
                                   scalar1=1.0 / NW, scalar2=FFN_EPS,
                                   op0=ALU.mult, op1=ALU.add)
        mf2 = E("vector", f2, waits=[("v", mf1)], inc="v")

        def f3(e, mi=mi):
            return e.reciprocal(out=g1[:, 8 + mi:9 + mi], in_=g1[:, 4 + mi:5 + mi])
        vm = E("vector", f3, waits=[("v", mf2)], inc="v")

        def f4(e, mi=mi):
            return e.matmul(out=pb[6][:, 256 + mi:257 + mi], lhsT=onesf,
                            rhs=g1[:, 8 + mi:9 + mi], start=True, stop=True)
        tm = E("tensor", f4, waits=[("v", vm), ("v", D2_V), ("dc", CONST_OK)], inc="t")

        def f5(e, mi=mi):
            return e.copy(out=ginv[:, mi:mi + 1], in_=pb[6][:, 256 + mi:257 + mi])
        ginv_last = E("scalar", f5, waits=[("t", tm)], inc="s")
    GINV_OK = ginv_last
    GF_T = M("t")

    NG = 11
    NSG = 22
    fq = [gst[:, 0:256], gst[:, 256:512], gst[:, 512:768], gst[:, 768:1024]]
    fq_sems = ["df0", "df1", "df2", "df3"]
    q12_marks = {}
    mm12_sub = {}
    w3q_marks = {}
    mm3_last = {}
    yadd_f = {}
    fq_use = [0, 0, 0, 0]
    dh_use = {0: 0, 1: 0}
    q3k = 0

    def w12_arena(sg):
        return [Aw[:, 0:8192], Aw[:, 8192:16384], Ax[:, 2048:10240]][sg % 3]

    fstg = [Ax32[:, 5120:6656], Ax32[:, 6656:8192]]
    fstg_use = [0, 0]
    fq_chunks = [(0, 6), (6, 6), (12, 4)]

    def emit_quant_sub(sg):
        ar = w12_arena(sg)
        ci = 0
        for mi, wt in enumerate((w1t, w2t)):
            for (t0, nt) in fq_chunks:
                sl = (sg * 6 + ci) % 2
                dsem = fq_sems[sl]
                w = []
                if sg <= 2 and ci == 0 and mi == 0:
                    w = [("t", D2_T), ("s", D2_S), ("v", D2_V)]
                if sg >= 3 and ci == 0 and mi == 0:
                    w.append(("t", mm12_sub[sg - 3]))
                qm_prev = q12_marks.get(("c", sg * 6 + ci - 2))
                if qm_prev:
                    w.append(("v", qm_prev))

                def fd(e, wt=wt, t0=t0, nt=nt, sg=sg, sl=sl):
                    return e.dma_start(
                        out=fstg[sl][:, 0:nt * 256].rearrange("p (t f) -> p t f", t=nt),
                        in_=wt[t0 * 128:(t0 + nt) * 128, sg * 256:(sg + 1) * 256]
                        .rearrange("(t p) f -> p t f", p=128))
                w.append((dsem, fstg_use[sl] * 16))
                fstg_use[sl] += 1
                dm16 = fstg_use[sl] * 16
                E("sync", fd, waits=w, inc=dsem, inc_n=16)

                qm = None
                for tl in range(nt):
                    t = t0 + tl
                    ws = [(dsem, dm16)] if tl == 0 else []
                    if sg == 0 and ci == 0 and mi == 0 and tl == 0:
                        ws.append(("s", GINV_OK))
                    if sg == 2 and ci == 0 and mi == 0 and tl == 0:
                        ws.append(("v", D2_V))
                    if sg >= 3 and ci == 0 and mi == 0 and tl == 0:
                        ws.append(("t", mm12_sub[sg - 3]))

                    def f1(e, sl=sl, tl=tl, mi=mi):
                        return e.activation(
                            out=fstg[sl][:, tl * 256:(tl + 1) * 256],
                            in_=fstg[sl][:, tl * 256:(tl + 1) * 256],
                            func=AF.Identity, scale=ginv[:, mi:mi + 1], bias=magicb)
                    s1 = E("scalar", f1, waits=ws, inc="s")

                    def f2(e, sl=sl, tl=tl):
                        return e.tensor_scalar(
                            out=fstg[sl][:, tl * 256:(tl + 1) * 256],
                            in0=fstg[sl][:, tl * 256:(tl + 1) * 256],
                            scalar1=MAGIC, scalar2=1.0,
                            op0=ALU.subtract, op1=ALU.min)
                    m2 = E("vector", f2, waits=[("s", s1)], inc="v")

                    def f3(e, sl=sl, tl=tl, t=t, mi=mi):
                        return e.tensor_scalar(
                            out=ar[:, mi * 4096 + t * 256:
                                   mi * 4096 + (t + 1) * 256],
                            in0=fstg[sl][:, tl * 256:(tl + 1) * 256],
                            scalar1=-1.0, scalar2=None, op0=ALU.max)
                    qm = E("vector", f3, waits=[("v", m2)], inc="v")
                q12_marks[("c", sg * 6 + ci)] = qm
                ci += 1
        q12_marks[sg] = qm

    def emit_quant_w3(gi):
        nonlocal_k = None
        qm = None
        for fi in range(4):
            for hh in range(2):
                k = gi * 8 + fi * 2 + hh
                sl = k % 2
                dsem = "dh0" if sl == 0 else "dh1"
                w = []
                if k == 0:
                    w = [("t", D2_T), ("s", D2_S)]
                qm_prev = q12_marks.get(("w3", k - 2))
                if qm_prev:
                    w.append(("v", qm_prev))
                if gi > 0 and fi == 0 and hh == 0:
                    w.append(("t", mm3_last[gi - 1]))

                def fd(e, gi=gi, fi=fi, hh=hh, sl=sl):
                    return e.dma_start(
                        out=hst[sl],
                        in_=w3t[(gi * 4 + fi) * 128:(gi * 4 + fi + 1) * 128,
                                hh * 1024:(hh + 1) * 1024])
                w.append((dsem, dh_use[sl] * 16))
                dh_use[sl] += 1
                dm16 = dh_use[sl] * 16
                E("sync", fd, waits=w, inc=dsem, inc_n=16)

                ws = [(dsem, dm16)]
                if k == 0:
                    ws.append(("s", GINV_OK))
                if gi > 0 and fi == 0 and hh == 0:
                    ws.append(("t", mm3_last[gi - 1]))

                def f1(e, sl=sl):
                    return e.activation(out=hst[sl], in_=hst[sl], func=AF.Identity,
                                        scale=ginv[:, 2:3], bias=magicb)
                s1 = E("scalar", f1, waits=ws, inc="s")

                def f2(e, sl=sl):
                    return e.tensor_scalar(out=hst[sl], in0=hst[sl],
                                           scalar1=MAGIC, scalar2=1.0,
                                           op0=ALU.subtract, op1=ALU.min)
                m2 = E("vector", f2, waits=[("s", s1)], inc="v")

                def f3(e, sl=sl, fi=fi, hh=hh):
                    return e.tensor_scalar(
                        out=Ac[:, fi * 2048 + hh * 1024: fi * 2048 + (hh + 1) * 1024],
                        in0=hst[sl], scalar1=-1.0, scalar2=None, op0=ALU.max)
                qm = E("vector", f3, waits=[("v", m2)], inc="v")
                q12_marks[("w3", k)] = qm
        w3q_marks[gi] = qm

    emit_quant_sub(0)
    emit_quant_sub(1)
    prev_group_y = None
    h_marks = {}
    sgm_marks = {}
    for gi in range(NG):
        emit_quant_w3(gi)
        for half in range(2):
            sg = gi * 2 + half
            mm2k = None
            for fl in range(2):
                fi = half * 2 + fl
                w = [("v", q12_marks[sg])]
                if gi == 0 and fi == 0:
                    w += [("t", GF_T), ("s", D2_S), ("v", D2_V)]
                prev_sgm = sgm_marks.get(fi - 1) if fi >= 1 else (
                    sgm_marks.get(3) if gi > 0 else None)
                if prev_sgm is not None:
                    w.append(("s", prev_sgm))
                for t in range(16):
                    for mh in range(2):
                        def fm1(e, t=t, mh=mh, fl=fl, sg=sg):
                            ar = w12_arena(sg)
                            return e.matmul(
                                out=pb[mh],
                                lhsT=ar[:, t * 256 + fl * 128:
                                        t * 256 + (fl + 1) * 128],
                                rhs=Aq[:, t * 1024 + mh * 512: t * 1024 + (mh + 1) * 512],
                                start=(t == 0), stop=(t == 15))
                        E("tensor", fm1, waits=(w if t == 0 and mh == 0 else ()))
                w2m = []
                prev_h = h_marks.get(fi - 1) if fi >= 1 else (
                    h_marks.get(3) if gi > 0 else None)
                if prev_h is not None:
                    w2m.append(("v", prev_h))
                for t in range(16):
                    for mh in range(2):
                        def fm2(e, t=t, mh=mh, fl=fl, sg=sg):
                            ar = w12_arena(sg)
                            return e.matmul(
                                out=pb[2 + mh],
                                lhsT=ar[:, 4096 + t * 256 + fl * 128:
                                        4096 + t * 256 + (fl + 1) * 128],
                                rhs=Aq[:, t * 1024 + mh * 512: t * 1024 + (mh + 1) * 512],
                                start=(t == 0), stop=(t == 15))
                        mm2k = E("tensor", fm2,
                                 waits=(w2m if t == 0 and mh == 0 else ()),
                                 inc="t" if (t == 15 and mh == 1) else None)
                ws = [("t", mm2k)]
                if fi >= 2:
                    ws.append(("v", h_marks[fi - 2]))

                def fsg0(e, sl=fi % 2):
                    return e.activation(out=sg_buf[sl][:, 0:512], in_=pb[0],
                                        func=AF.Sigmoid)

                def fsg1(e, sl=fi % 2):
                    return e.activation(out=sg_buf[sl][:, 512:1024], in_=pb[1],
                                        func=AF.Sigmoid)
                E("scalar", fsg0, waits=ws, inc="s")
                sgm = E("scalar", fsg1, inc="s")
                sgm_marks[fi] = sgm

                def fh0(e, fi=fi, sl=fi % 2):
                    return e.tensor_tensor(out=hgr[:, fi * 1024: fi * 1024 + 512],
                                           in0=sg_buf[sl][:, 0:512], in1=pb[2],
                                           op=ALU.mult)

                def fh1(e, fi=fi, sl=fi % 2):
                    return e.tensor_tensor(out=hgr[:, fi * 1024 + 512: (fi + 1) * 1024],
                                           in0=sg_buf[sl][:, 512:1024], in1=pb[3],
                                           op=ALU.mult)
                wv = [("s", sgm)]
                if gi > 0 and fi < 2:
                    wv.append(("t", mm3_last[gi - 1]))
                E("vector", fh0, waits=wv, inc="v")
                hm = E("vector", fh1, inc="v")
                h_marks[fi] = hm
            mm12_sub[sg] = mm2k
            if sg + 2 <= NSG - 1:
                emit_quant_sub(sg + 2)
        H_DONE = M("v")

        for mt in range(8):
            mmk = None
            for half3 in range(2):
                for fi in range(4):
                    w = [("v", h_marks[fi])]
                    if fi == 0:
                        w.append(("v", h_marks[3]))
                        w.append(("v", w3q_marks[gi]))
                        ya = yadd_f.get((gi, mt - 1, half3)) or \
                            (yadd_f.get((gi - 1, 7, half3)) if mt == 0 else None)
                        if ya is not None:
                            w.append(("v", ya))
                    for nch in (half3 * 2, half3 * 2 + 1):
                        def fm(e, mt=mt, fi=fi, nch=nch):
                            return e.matmul(
                                out=pb[4 + nch],
                                lhsT=hgr[:, fi * 1024 + mt * 128: fi * 1024 + (mt + 1) * 128],
                                rhs=Ac[:, fi * 2048 + nch * 512: fi * 2048 + (nch + 1) * 512],
                                start=(fi == 0), stop=(fi == 3))
                        mmk = E("tensor", fm,
                                waits=(w if nch == half3 * 2 and fi == 0 else ()),
                                inc="t" if (fi == 3 and nch == half3 * 2 + 1) else None)
                ym = None
                for nch in (half3 * 2, half3 * 2 + 1):
                    def fy(e, mt=mt, nch=nch):
                        return e.tensor_tensor(
                            out=Ay[:, mt * 2048 + nch * 512: mt * 2048 + (nch + 1) * 512],
                            in0=Ay[:, mt * 2048 + nch * 512: mt * 2048 + (nch + 1) * 512],
                            in1=pb[4 + nch], op=ALU.add)
                    ym = E("vector", fy,
                           waits=([("t", mmk)] if nch == half3 * 2 else ()), inc="v")
                yadd_f[(gi, mt, half3)] = ym

            if gi == NG - 1:
                def fo(e, mt=mt):
                    return e.dma_start(out=out[mt * 128:(mt + 1) * 128, :],
                                       in_=Ay[:, mt * 2048:(mt + 1) * 2048])
                E("sync", fo, waits=[("v", yadd_f[(gi, mt, 1)])], inc="dob", inc_n=16)
        mm3_last[gi] = mmk
        prev_group_y = yadd_f[(gi, 7, 1)]

    E("sync", lambda e: e.nop(), waits=[("dob", M("dob"))])

    with nc.Block() as block:
        def run(eng_obj, name):
            hwm = {}
            for waits, fn, inc in P.stream[name]:
                for (s, v) in waits:
                    if v is None:
                        continue
                    if v > hwm.get(s, 0):
                        eng_obj.wait_ge(P.sems[s], v)
                        hwm[s] = v
                inst = fn(eng_obj)
                if inc is not None:
                    inst.then_inc(P.sems[inc[0]], inc[1])

        @block.sync
        def _(e):
            run(e, "sync")

        @block.gpsimd
        def _(e):
            run(e, "gpsimd")

        @block.vector
        def _(e):
            run(e, "vector")

        @block.scalar
        def _(e):
            run(e, "scalar")

        @block.tensor
        def _(e):
            run(e, "tensor")

    return nc


_NC_CACHE = None


def kernel(L0_attn_x, L0_attn_w_q, L0_attn_w_kv, L0_attn_w_dense,
           L0_ffn_w1, L0_ffn_w2, L0_ffn_w3):
    global _NC_CACHE
    x = np.asarray(L0_attn_x, np.float32)
    wq = np.ascontiguousarray(np.asarray(L0_attn_w_q, np.float32))
    wkv = np.ascontiguousarray(np.asarray(L0_attn_w_kv, np.float32))
    wd = np.ascontiguousarray(np.asarray(L0_attn_w_dense, np.float32))
    w1t = np.ascontiguousarray(np.asarray(L0_ffn_w1, np.float32).T)
    w2t = np.ascontiguousarray(np.asarray(L0_ffn_w2, np.float32).T)
    w3t = np.ascontiguousarray(np.asarray(L0_ffn_w3, np.float32).T)

    ident = np.eye(128, dtype=ml_dtypes.bfloat16)
    onesb = np.ones((128, 1), dtype=ml_dtypes.bfloat16)
    onesf = np.ones((1, 128), dtype=np.float32)
    kposd = (np.arange(128, dtype=np.float32)[:, None]
             + 128.0 * np.arange(32, dtype=np.float32)[None, :])
    magicv = np.full((128, 1), MAGIC, dtype=np.float32)

    if _NC_CACHE is None:
        _NC_CACHE = build_nc()
    nc = _NC_CACHE

    in_maps = []
    xbt_all = [np.ascontiguousarray(x[b].T) for b in range(B)]
    for c in range(8):
        b, j = c // 4, c % 4
        blocks = SLOT_BLOCKS[j]
        xo = np.concatenate([x[b, 256 * bs:256 * bs + 256] for bs in blocks], axis=0)
        qp = np.concatenate([256.0 * bs + np.arange(256, dtype=np.float32)
                             for bs in blocks])
        qposd = np.ascontiguousarray(np.broadcast_to(qp[None, :], (128, NQO)))
        in_maps.append(dict(
            xbt=xbt_all[b], xot=np.ascontiguousarray(xo.T),
            xo=np.ascontiguousarray(xo),
            wq=wq, wkv=wkv, wd=wd, w1t=w1t, w2t=w2t, w3t=w3t,
            qposd=qposd, kposd=kposd, identd=ident, onesbd=onesb, onesfd=onesf,
            magicd=magicv,
        ))

    res = run_bass_kernel_spmd(nc, in_maps, core_ids=list(range(8)))

    y = np.empty((B, SQ, H), np.float32)
    for c in range(8):
        b, j = c // 4, c % 4
        oc = np.asarray(res.results[c]["out"])
        for s, bs in enumerate(SLOT_BLOCKS[j]):
            y[b, 256 * bs:256 * bs + 256] = oc[256 * s:256 * s + 256]
    return y


# version 15
# speedup vs baseline: 1.1102x; 1.1102x over previous
import numpy as np
import ml_dtypes
import concourse.bass as bass
import concourse.mybir as mybir
from concourse.bass_utils import run_bass_kernel_spmd

F32 = mybir.dt.float32
BF16 = mybir.dt.bfloat16
AF = mybir.ActivationFunctionType
ALU = mybir.AluOpType
AX = mybir.AxisListType

B, SQ, H = 2, 4096, 2048
HEADS, G, KVC = 16, 4, 128
FF = 5632
NQO = 1024
SCALE = float(128 ** -0.5)
MAGIC = float(np.float32(12582912.0))
LN_EPS = 1e-5
FFN_EPS = 1e-8
NW = float(FF * H)
PK = [32, 24, 16, 8]
SLOT_BLOCKS = [
    [15, 10, 5, 0],
    [14, 11, 4, 1],
    [13, 8, 7, 2],
    [12, 9, 6, 3],
]
NPT = SQ // 128
GCH = 1536


class Prog:
    ENGS = ("sync", "gpsimd", "vector", "scalar", "tensor")

    def __init__(self, nc):
        self.nc = nc
        self.stream = {e: [] for e in self.ENGS}
        self.cnt = {}
        self.sems = {}

    def sem(self, name):
        self.sems[name] = self.nc.alloc_semaphore(name)
        self.cnt[name] = 0

    def emit(self, eng, fn, waits=(), inc=None, inc_n=1):
        entry = [list(waits), fn, (inc, inc_n) if inc else None]
        self.stream[eng].append(entry)
        if inc:
            self.cnt[inc] += inc_n
            return self.cnt[inc]
        return None

    def reserve(self, eng, inc, inc_n):
        entry = [[], None, (inc, inc_n)]
        self.stream[eng].append(entry)
        self.cnt[inc] += inc_n
        return entry, self.cnt[inc]

    def mark(self, name):
        return self.cnt[name]


def build_nc():
    nc = bass.Bass("TRN2", target_bir_lowering=False)

    xbt = nc.dram_tensor("xbt", [H, SQ], F32, kind="ExternalInput").ap()
    xot = nc.dram_tensor("xot", [H, NQO], F32, kind="ExternalInput").ap()
    xo = nc.dram_tensor("xo", [NQO, H], F32, kind="ExternalInput").ap()
    wq = nc.dram_tensor("wq", [H, H], F32, kind="ExternalInput").ap()
    wkv = nc.dram_tensor("wkv", [H, 2 * G * KVC], F32, kind="ExternalInput").ap()
    wd = nc.dram_tensor("wd", [H, H], F32, kind="ExternalInput").ap()
    w1t = nc.dram_tensor("w1t", [H, FF], F32, kind="ExternalInput").ap()
    w2t = nc.dram_tensor("w2t", [H, FF], F32, kind="ExternalInput").ap()
    w3t = nc.dram_tensor("w3t", [FF, H], F32, kind="ExternalInput").ap()
    qposd = nc.dram_tensor("qposd", [128, NQO], F32, kind="ExternalInput").ap()
    kposd = nc.dram_tensor("kposd", [128, 32], F32, kind="ExternalInput").ap()
    identd = nc.dram_tensor("identd", [128, 128], BF16, kind="ExternalInput").ap()
    onesbd = nc.dram_tensor("onesbd", [128, 1], BF16, kind="ExternalInput").ap()
    onesfd = nc.dram_tensor("onesfd", [1, 128], F32, kind="ExternalInput").ap()
    magicd = nc.dram_tensor("magicd", [128, 1], F32, kind="ExternalInput").ap()
    out = nc.dram_tensor("out", [NQO, H], F32, kind="ExternalOutput").ap()
    gscr = nc.dram_tensor("gscr", [128, 3], F32, kind="Internal").ap()

    def sb(name, shape, dt=F32):
        return nc.alloc_sbuf_tensor(name, shape, dt).ap()

    Ay = sb("Ay", [128, 16384], F32)
    Ay_b = Ay.bitcast(BF16)
    kT = Ay_b[:, 0:16384]
    kT_r = kT.rearrange("p (g t) -> p g t", g=4)
    vsb = Ay_b[:, 16384:32768]
    Aq = sb("Aq", [128, 16384], BF16)
    Aw = sb("Aw", [128, 16384], BF16)
    Ac = sb("Ac", [128, 8192], BF16)
    cbuf = sb("cbuf", [128, 4224], BF16)
    cb32 = cbuf.bitcast(F32)
    qpos = cb32[:, 0:1024]
    binv = cb32[:, 1024:2048]
    kpos = cb32[:, 2048:2080]
    expb = [Ac[:, 1024:1536], Ac[:, 1536:2048], Ac[:, 2048:2560],
            Ac[:, 6656:7168], Ac[:, 7168:7680], Ac[:, 7680:8192]]
    expm = [Ac[:, 2560 + 512 * i: 3072 + 512 * i] for i in range(8)]
    hgr = cbuf[:, 0:4096]

    Ax = sb("Ax", [128, 16384], BF16)
    Ax32 = Ax.bitcast(F32)
    xst = [Ax32[:, 0:2048], Ax32[:, 2048:4096]]
    xTg = [Ax[:, 0:8192], Ax[:, 8192:16384]]
    gst = sb("gst", [128, 2 * GCH], F32)
    fst = [gst[:, 0:512], gst[:, 512:1024]]
    hst = [gst[:, 1024:2048], gst[:, 2048:3072]]
    xlnb = [cbuf[:, 0:2048], cbuf[:, 2048:4096]]
    Ac32 = Ac.bitcast(F32)
    mb = [Ac32[:, 0:512], Ac32[:, 512:1024]]
    rb = [Ac32[:, 1024:1536], Ac32[:, 1536:2048]]
    ctmp = [Ac32[:, 2048:2560], Ac32[:, 2560:3072]]
    csVb = Ac32[:, 3072:3584]
    x2b = [Ac[:, 7168:7680], Ac[:, 7680:8192]]
    rows1 = sb("rows1", [128, 1024], F32)
    rows2 = sb("rows2", [128, 1024], F32)
    mrow = rows1[0:1, 0:512]
    nrow = rows1[0:1, 512:1024]
    e2row = rows2[0:1, 0:512]
    m2row = rows2[0:1, 512:1024]
    sdrow = rows2[0:1, 0:512]
    mcol = [sb("mcol0", [128, 8], F32), sb("mcol1", [128, 8], F32)]
    csKc = sb("csKc", [128, 4], F32)
    csQc = sb("csQc", [128, 16], F32)
    sg_buf = [Ax[:, 0:1024], Ax[:, 1024:2048]]
    riv = rows1[0:1, 0:1024]
    bn24 = sb("bn24", [128, 24], F32)
    stats = sb("stats", [128, 4], F32)
    vtmp = sb("vtmp", [128, 8], F32)
    redcol = sb("redcol", [128, 224], F32)
    gsum = sb("gsum", [128, 4], F32)
    grow = rows2[0:1, 512:1024]
    g1 = sb("g1", [1, 16], F32)
    ginv = sb("ginv", [128, 4], F32)
    ident = sb("ident", [128, 128], BF16)
    onesb = sb("onesb", [128, 1], BF16)
    onesf = sb("onesf", [1, 128], F32)
    magicb = sb("magicb", [128, 1], F32)

    pb = [nc.alloc_psum_tensor(f"pb{i}", [128, 512], F32).ap() for i in range(8)]
    pbh = [p.bitcast(BF16) for p in pb]

    P = Prog(nc)
    for s in ("dc", "dx0", "dx1", "dkv", "dq0", "dq1", "dq2", "dq3",
              "dd0", "dd1", "dg0x", "dg1x", "dg0", "dg1", "df0", "df1", "df2", "df3",
              "dh0", "dh1", "dgb", "dob", "v", "s", "t", "g", "gv", "gsc"):
        P.sem(s)
    E = P.emit
    M = P.mark

    E("sync", lambda e: e.dma_start(out=qpos, in_=qposd), inc="dc", inc_n=16)
    E("sync", lambda e: e.dma_start(out=kpos, in_=kposd), inc="dc", inc_n=16)
    E("sync", lambda e: e.dma_start(out=ident, in_=identd), inc="dc", inc_n=16)
    E("sync", lambda e: e.dma_start(out=onesb, in_=onesbd), inc="dc", inc_n=16)
    E("sync", lambda e: e.dma_start(out=onesf, in_=onesfd), inc="dc", inc_n=16)
    E("sync", lambda e: e.dma_start(out=magicb, in_=magicd), inc="dc", inc_n=16)
    CONST_OK = M("dc")

    wkv_r = wkv.rearrange("(t p) e -> p t e", p=128)
    wq_r = wq.rearrange("(t p) e -> p t e", p=128)
    wd_r = wd.rearrange("(t p) e -> p t e", p=128)

    def cast_wq2(qq):
        def f(e, qq=qq):
            return e.dma_start(out=Aw.rearrange("p (t e) -> p t e", t=16),
                               in_=wq_r[:, :, qq * 512:(qq + 2) * 512])
        return f

    def cast_wd(h):
        def f(e, h=h):
            return e.dma_start(out=Aw.rearrange("p (t e) -> p t e", t=16),
                               in_=wd_r[:, :, h * 1024:(h + 1) * 1024])
        return f

    def cast_xg(tg, sl):
        def f(e, tg=tg, sl=sl):
            return e.dma_start(out=xTg[sl].rearrange("p (t m) -> p t m", t=16),
                               in_=xbt[:, tg * 512:(tg + 1) * 512]
                               .rearrange("(t p) m -> p t m", p=128))
        return f

    def cast_xo(grp, sl):
        def f(e, grp=grp, sl=sl):
            return e.dma_start(out=xTg[sl].rearrange("p (t m) -> p t m", t=16),
                               in_=xot[:, grp * 512:(grp + 1) * 512]
                               .rearrange("(t p) m -> p t m", p=128))
        return f

    E("gpsimd", lambda e: e.dma_start(out=Aw.rearrange("p (t e) -> p t e", t=16),
                                      in_=wkv_r), inc="dkv", inc_n=16)
    E("gpsimd", cast_xg(0, 0), inc="dq0", inc_n=16)
    E("gpsimd", cast_xg(1, 1), inc="dq1", inc_n=16)
    xg_sem = {0: ("dq0", 16), 1: ("dq1", 16)}

    gchunks = []
    for mi, wt in enumerate((w1t, w2t)):
        for t in range(16):
            for c0 in range(0, FF, GCH):
                cw = min(GCH, FF - c0)
                gchunks.append((mi, wt[t * 128:(t + 1) * 128, c0:c0 + cw], cw))
    for t in range(44):
        for c0 in (0, 1024):
            gchunks.append((2, w3t[t * 128:(t + 1) * 128, c0:c0 + 1024], 1024))
    mat_cols = {0: [0, 63], 1: [64, 127], 2: [128, 215]}

    gpend = []
    for ci, (mi, src_, cw) in enumerate(gchunks):
        sl = ci % 2
        dsem = "dg0" if sl == 0 else "dg1"
        psem = "gv" if sl == 0 else "gsc"
        waits = [(dsem, (ci // 2) * 16)] if ci >= 2 else []
        if ci >= 2:
            waits.append((psem, ci // 2))

        def fd(e, src_=src_, cw=cw, sl=sl):
            return e.dma_start(out=gst[:, sl * GCH: sl * GCH + cw], in_=src_)
        gm = E("sync", fd, waits=waits, inc=dsem, inc_n=16)

        if sl == 0:
            def fr(e, cw=cw, sl=sl, ci=ci):
                return e.reduce_sum(out=redcol[:, ci:ci + 1],
                                    in_=gst[:, sl * GCH: sl * GCH + cw],
                                    axis=AX.X, apply_absolute_value=True)
            gpend.append((ci, "vector", fr, (dsem, gm), psem))
        else:
            def fr(e, cw=cw, sl=sl, ci=ci):
                return e.activation(out=gst[:, sl * GCH: sl * GCH + cw],
                                    in_=gst[:, sl * GCH: sl * GCH + cw],
                                    func=AF.Abs,
                                    accum_out=redcol[:, ci:ci + 1])
            gpend.append((ci, "scalar", fr, (dsem, gm), psem))

    def flush_gamma(k, limit):
        m = 0
        while gpend and m < k and gpend[0][0] <= limit:
            ci_, eng, fr, wt, psem = gpend.pop(0)
            E(eng, fr, waits=[wt], inc=psem)
            m += 1

    GAMMA_BOUNCE = [None, None, None]


    mscr = nc.dram_tensor("mscr", [2, 2, 512], F32, kind="Internal").ap()

    B_SQ_SET = "exp_and_others"

    w = [("dkv", 16), ("dc", CONST_OK)]
    for g4 in range(4):
        for t in range(16):
            def fcs(e, g4=g4, t=t):
                return e.matmul(out=pb[6][:, g4:g4 + 1],
                                lhsT=Aw[:, t * 1024 + g4 * 128: t * 1024 + (g4 + 1) * 128],
                                rhs=onesb, start=(t == 0), stop=(t == 15))
            csm = E("tensor", fcs, waits=(w if g4 == 0 and t == 0 else ()),
                    inc="t" if t == 15 else None)
    def fcse(e):
        return e.copy(out=csKc, in_=pb[6][:, 0:4])
    cske = E("scalar", fcse, waits=[("t", csm)], inc="s")
    for t in range(16):
        def fcv(e, t=t):
            return e.matmul(out=pb[7][0:1, 0:512], lhsT=onesb,
                            rhs=Aw[:, t * 1024 + 512: (t + 1) * 1024],
                            start=(t == 0), stop=(t == 15))
        cvm = E("tensor", fcv, inc="t" if t == 15 else None)
    def fcve(e):
        return e.copy(out=sdrow, in_=pb[7][0:1, 0:512])
    cve = E("scalar", fcve, waits=[("t", cvm)], inc="s")
    def fcvb(e):
        return e.matmul(out=pb[7][:, 0:512], lhsT=onesf, rhs=sdrow,
                        start=True, stop=True)
    cvbm = E("tensor", fcvb, waits=[("s", cve)], inc="t")
    def fcvbe(e):
        return e.copy(out=csVb, in_=pb[7][:, 0:512])
    cvbe = E("scalar", fcvbe, waits=[("t", cvbm)], inc="s")

    bank_free = {}
    stats_marks = {}
    kv_done = {}
    corr_done = {}
    x2_marks = {}
    stats_state = {"colm": None, "ce": None}

    def emit_stats_sums(tg, after_w=()):
        sl = tg % 2
        wsq = list(after_w)
        if stats_state["ce"] is not None:
            wsq.append(("s", stats_state["ce"]))
        smm = None
        for t in range(16):
            def fsm(e, t=t, sl=sl):
                return e.matmul(out=pb[6][0:1, 0:512], lhsT=onesb,
                                rhs=xTg[sl][:, t * 512:(t + 1) * 512],
                                start=(t == 0), stop=(t == 15))
            smm = E("tensor", fsm, waits=(wsq if t == 0 else ()),
                    inc="t" if t == 15 else None)
        stats_state["smm"] = smm
        stats_state["sums_w"] = list(after_w)

    def emit_stats_sq(tg):
        sl = tg % 2
        after_w = stats_state["sums_w"]
        sqm = None
        for t in range(16):
            def fx2(e, t=t, sl=sl):
                return e.activation(out=x2b[t % 2],
                                    in_=xTg[sl][:, t * 512:(t + 1) * 512],
                                    func=AF.Square)
            wq2 = list(after_w) if t == 0 else []
            if t == 1 and stats_state["ce"] is not None:
                pass
            if t >= 2:
                wq2.append(("t", x2_marks[(tg, t - 2)]))
            s2 = E("scalar", fx2, waits=wq2, inc="s")

            def fsq(e, t=t):
                return e.matmul(out=pb[7][0:1, 0:512], lhsT=onesb, rhs=x2b[t % 2],
                                start=(t == 0), stop=(t == 15))
            wq3 = [("s", s2)]
            if t == 0 and stats_state["ce"] is not None:
                wq3.append(("s", stats_state["ce"]))
            sqm = E("tensor", fsq, waits=wq3, inc="t")
            x2_marks[(tg, t)] = sqm
        stats_state["sqm"] = sqm

    def emit_stats_rows(tg, mbuser=None):
        sl = tg % 2
        smm, sqm = stats_state["smm"], stats_state["sqm"]
        wfm = [("t", smm), ("t", sqm)]
        if stats_state["colm"] is not None:
            wfm.append(("t", stats_state["colm"]))

        def f_m(e):
            return e.tensor_scalar(out=mrow, in0=pb[6][0:1, 0:512],
                                   scalar1=1.0 / 2048.0, scalar2=None, op0=ALU.mult)
        vm1 = E("vector", f_m, waits=wfm, inc="v")

        def f_e2(e):
            return e.tensor_scalar(out=e2row, in0=pb[7][0:1, 0:512],
                                   scalar1=1.0 / 2048.0, scalar2=None, op0=ALU.mult)
        vm2 = E("vector", f_e2, waits=[("v", vm1)], inc="v")

        def f_m2(e):
            return e.tensor_tensor(out=m2row, in0=mrow, in1=mrow, op=ALU.mult)
        vm3 = E("vector", f_m2, waits=[("v", vm2)], inc="v")

        def f_ve(e):
            return e.scalar_tensor_tensor(out=m2row, in0=e2row, scalar=LN_EPS,
                                          in1=m2row, op0=ALU.add, op1=ALU.subtract)
        vm4 = E("vector", f_ve, waits=[("v", vm3)], inc="v")

        def f_sd(e):
            return e.activation(out=sdrow, in_=m2row, func=AF.Sqrt)
        sm1 = E("scalar", f_sd, waits=[("v", vm4)], inc="s")

        def f_rc(e):
            return e.reciprocal(out=nrow, in_=sdrow)
        vm5 = E("vector", f_rc, waits=[("s", sm1)], inc="v")

        def f_ng(e):
            return e.tensor_scalar(out=nrow, in0=nrow, scalar1=-1.0,
                                   scalar2=None, op0=ALU.mult)
        vm6 = E("vector", f_ng, waits=[("v", vm5)], inc="v")

        wbm = [("v", vm6)]
        if mbuser is not None:
            wbm.append(("v", mbuser))

        def f_bm(e):
            return e.matmul(out=pb[6][:, 0:512], lhsT=onesf, rhs=mrow,
                            start=True, stop=True)
        bm = E("tensor", f_bm, waits=wbm, inc="t")

        def f_bn(e):
            return e.matmul(out=pb[7][:, 0:512], lhsT=onesf, rhs=nrow,
                            start=True, stop=True)
        bn_ = E("tensor", f_bn, inc="t")

        def f_em(e, sl=sl):
            return e.copy(out=mb[sl], in_=pb[6][:, 0:512])
        em1 = E("scalar", f_em, waits=[("t", bm)], inc="s")

        def f_en(e, sl=sl):
            return e.copy(out=rb[sl], in_=pb[7][:, 0:512])
        em2 = E("scalar", f_en, waits=[("t", bn_)], inc="s")
        colm = None
        for tt in range(4):
            def f_cm(e, tt=tt):
                return e.matmul(out=pb[6][:, 8 + tt:9 + tt],
                                lhsT=mrow[0:1, tt * 128:(tt + 1) * 128],
                                rhs=onesf[0:1, 0:1], start=True, stop=True)
            colm = E("tensor", f_cm, waits=([("s", em1)] if tt == 0 else ()),
                     inc="t" if tt == 3 else None)

            def f_cn(e, tt=tt):
                return e.matmul(out=pb[6][:, 12 + tt:13 + tt],
                                lhsT=nrow[0:1, tt * 128:(tt + 1) * 128],
                                rhs=onesf[0:1, 0:1], start=True, stop=True)
            colm = E("tensor", f_cn, inc="t" if tt == 3 else None)

        def f_ce(e, sl=sl):
            return e.copy(out=mcol[sl], in_=pb[6][:, 8:16])
        ce = E("scalar", f_ce, waits=[("t", colm)], inc="s")
        stats_state["colm"] = colm
        stats_state["ce"] = ce
        return dict(bcast=(em1, em2), col=ce, vm=vm6)

    def emit_stats(tg, after_w=(), mbuser=None):
        emit_stats_sums(tg, after_w)
        emit_stats_sq(tg)
        return emit_stats_rows(tg, mbuser)

    prev_corr = {}
    for tg in range(8):
        sl = tg % 2
        if tg == 0:
            stats_marks[0] = emit_stats(0, after_w=[xg_sem[0], ("s", cvbe)],
                                        mbuser=None)
        st = stats_marks[tg]
        if tg + 1 <= 7:
            emit_stats_sums(tg + 1, after_w=[xg_sem[tg + 1]])
        kw = [("s", st["bcast"][1]), ("s", st["col"])]
        km = None
        for g4 in range(4):
            kwg = list(kw) if g4 == 0 else []
            if tg >= 1:
                kwg.append(("v", bank_free[(tg - 1, g4)]))
            for t in range(16):
                def fk(e, g4=g4, t=t, sl=sl):
                    return e.matmul(out=pb[g4],
                                    lhsT=Aw[:, t * 1024 + g4 * 128: t * 1024 + (g4 + 1) * 128],
                                    rhs=xTg[sl][:, t * 512:(t + 1) * 512],
                                    start=(t == 0), stop=(t == 15))
                km = E("tensor", fk, waits=(kwg if t == 0 else ()),
                       inc="t" if t == 15 else None)
            def fc1(e, g4=g4, sl=sl):
                return e.scalar_tensor_tensor(out=ctmp[g4 % 2], in0=mb[sl],
                                              scalar=csKc[:, g4:g4 + 1],
                                              in1=pb[g4], op0=ALU.mult,
                                              op1=ALU.subtract)
            wc = [("t", km)]
            if g4 >= 2:
                wc.append(("v", prev_corr[(tg, g4 - 2)]))
            c1 = E("vector", fc1, waits=wc, inc="v")
            bank_free[(tg, g4)] = c1
            def fc2(e, g4=g4, sl=sl, tg=tg):
                return e.tensor_tensor(out=kT[:, g4 * 4096 + tg * 512:
                                               g4 * 4096 + (tg + 1) * 512],
                                       in0=ctmp[g4 % 2], in1=rb[sl], op=ALU.mult)
            c2 = E("vector", fc2, waits=[("v", c1)], inc="v")
            prev_corr[(tg, g4)] = c2
            flush_gamma(2, [17, 33, 49, 65, 81, 97, 113, 115][tg])
        if tg + 1 <= 7:
            emit_stats_sq(tg + 1)
        vlast = None
        for tt in range(4):
            vb = 4 + tt % 2
            vw = []
            if tt >= 2:
                vw.append(("v", bank_free[("v", tg, tt - 2)]))
            elif tg >= 1:
                vw.append(("v", bank_free[("v", tg - 1, tt + 2)]))
            for t in range(16):
                def fv(e, tt=tt, t=t, sl=sl, vb=vb):
                    return e.matmul(out=pb[vb],
                                    lhsT=xTg[sl][:, t * 512 + tt * 128: t * 512 + (tt + 1) * 128],
                                    rhs=Aw[:, t * 1024 + 512: t * 1024 + 1024],
                                    start=(t == 0), stop=(t == 15))
                vm_ = E("tensor", fv, waits=(vw if t == 0 else ()),
                        inc="t" if t == 15 else None)
            def fvc1(e, tt=tt, sl=sl, vb=vb):
                return e.scalar_tensor_tensor(out=ctmp[tt % 2], in0=csVb,
                                              scalar=mcol[sl][:, tt:tt + 1],
                                              in1=pb[vb], op0=ALU.mult,
                                              op1=ALU.subtract)
            v1 = E("vector", fvc1,
                   waits=[("t", vm_)] + ([("v", prev_corr[("v", tg, tt - 2)])]
                                         if tt >= 2 else []), inc="v")
            bank_free[("v", tg, tt)] = v1
            def fvc2(e, tt=tt, sl=sl, tg=tg):
                return e.tensor_scalar(out=vsb[:, (tg * 4 + tt) * 512:
                                                (tg * 4 + tt + 1) * 512],
                                       in0=ctmp[tt % 2],
                                       scalar1=mcol[sl][:, 4 + tt:5 + tt],
                                       scalar2=None, op0=ALU.mult)
            v2 = E("vector", fvc2, waits=[("v", v1)], inc="v")
            prev_corr[("v", tg, tt)] = v2
            vlast = v2
            flush_gamma(2, [17, 33, 49, 65, 81, 97, 113, 115][tg])
        kv_done[tg] = M("t")
        corr_done[tg] = vlast
        if tg + 1 <= 7:
            stats_marks[tg + 1] = emit_stats_rows(tg + 1,
                                                  mbuser=corr_done.get(tg - 1))

        if tg + 2 <= 7:
            sem = "dq0" if sl == 0 else "dq1"
            mk = E("gpsimd", cast_xg(tg + 2, sl), waits=[("v", vlast)],
                   inc=sem, inc_n=16)
            xg_sem[tg + 2] = (sem, mk)
        elif tg + 2 <= 9:
            sem = "dq2" if tg + 2 == 8 else "dq3"
            mk = E("gpsimd", cast_xo(tg + 2 - 8, sl), waits=[("v", vlast)],
                   inc=sem, inc_n=16)
            xg_sem[tg + 2] = (sem, mk)

    B_KV_T = M("t")
    E("gpsimd", cast_wq2(0), waits=[("t", B_KV_T)], inc="dd0", inc_n=16)
    WQ01 = ("dd0", 16)

    def emit_csq(h0):
        cwq = [WQ01 if h0 == 0 else WQ23]
        if stats_state["ce"] is not None:
            cwq.append(("s", stats_state["ce"]))
        cm = None
        for hh in range(8):
            h = h0 + hh
            aw_off = ((h // 4) % 2) * 512 + (h % 4) * 128
            for t in range(16):
                def fcq(e, t=t, aw_off=aw_off, hh=hh):
                    return e.matmul(out=pb[6][:, hh:hh + 1],
                                    lhsT=Aw[:, t * 1024 + aw_off:
                                            t * 1024 + aw_off + 128],
                                    rhs=onesb, start=(t == 0), stop=(t == 15))
                cm = E("tensor", fcq, waits=(cwq if hh == 0 and t == 0 else ()),
                       inc="t" if t == 15 else None)

        def fcqe(e, h0=h0):
            return e.copy(out=csQc[:, h0:h0 + 8], in_=pb[6][:, 0:8])
        return E("scalar", fcqe, waits=[("t", cm)], inc="s")

    csq_mark = {0: emit_csq(0), 8: None}
    q_marks = {}
    st8 = emit_stats(8, after_w=[xg_sem[8], ("s", csq_mark[0])],
                     mbuser=corr_done.get(6))
    st9 = emit_stats(9, after_w=[xg_sem[9]], mbuser=corr_done.get(7))
    sts = {0: st8, 1: st9}
    for h in range(16):
        flush_gamma(3 if h < 8 else 1, 135 if h < 8 else 194)
        if h == 8:
            csq_mark[8] = emit_csq(8)
        qq = h // 4
        aw_off = (qq % 2) * 512 + (h % 4) * 128
        for grp in range(2):
            sl = grp
            st = sts[grp]
            qb = grp
            wqm = [("s", st["bcast"][1]), ("s", st["col"])]
            wqm.append(WQ01 if qq < 2 else WQ23)
            if h >= 8:
                wqm.append(("s", csq_mark[8]))
            if h == 0 and grp == 0:
                wqm.append(("v", corr_done[7]))
            prev2 = bank_free.get(("q", h - 1, grp))
            if prev2 is not None:
                wqm.append(("v", prev2))
            qm_ = None
            for t in range(16):
                def fq_(e, t=t, sl=sl, aw_off=aw_off, qb=qb):
                    return e.matmul(out=pb[qb],
                                    lhsT=Aw[:, t * 1024 + aw_off:
                                            t * 1024 + aw_off + 128],
                                    rhs=xTg[sl][:, t * 512:(t + 1) * 512],
                                    start=(t == 0), stop=(t == 15))
                qm_ = E("tensor", fq_, waits=(wqm if t == 0 else ()),
                        inc="t" if t == 15 else None)

            def fqc1(e, h=h, sl=sl, qb=qb):
                return e.scalar_tensor_tensor(out=ctmp[qb], in0=mb[sl],
                                              scalar=csQc[:, h:h + 1],
                                              in1=pb[qb], op0=ALU.mult,
                                              op1=ALU.subtract)
            wq1 = [("t", qm_)]
            if q_marks.get((h - 1, grp)) is not None:
                wq1.append(("v", q_marks[(h - 1, grp)]))
            q1 = E("vector", fqc1, waits=wq1, inc="v")
            bank_free[("q", h, grp)] = q1

            def fqc2(e, h=h, sl=sl, grp=grp, qb=qb):
                return e.tensor_tensor(out=Aq[:, h * 1024 + grp * 512:
                                              h * 1024 + (grp + 1) * 512],
                                       in0=ctmp[qb], in1=rb[sl], op=ALU.mult)
            q2 = E("vector", fqc2, waits=[("v", q1)], inc="v")
            q_marks[(h, grp)] = q2
            if h == 7 and grp == 1:
                E("gpsimd", cast_wq2(2), waits=[("v", q2)], inc="dd1", inc_n=16)
                WQ23 = ("dd1", 16)

    B2_PE_DONE = M("t")
    B2_S_DONE = M("s")
    B2_V_DONE = M("v")

    def ln_chain(src_ap, sl, xdep):
        m4 = None
        for i in range(4):
            def f(e, i=i, src_ap=src_ap):
                return e.bn_stats(out=bn24[:, i * 6:(i + 1) * 6],
                                  in_=src_ap[:, i * 512:(i + 1) * 512])
            m4 = E("vector", f, waits=(xdep if i == 0 else ()), inc="v")

        def fa(e, sl=sl):
            return e.bn_aggr(out=stats[:, sl * 2:sl * 2 + 2], in_=bn24)
        ma = E("vector", fa, waits=[("v", m4)], inc="v")

        def fe(e, sl=sl):
            return e.tensor_scalar(out=vtmp[:, sl * 4:sl * 4 + 1],
                                   in0=stats[:, sl * 2 + 1:sl * 2 + 2],
                                   scalar1=LN_EPS, scalar2=None, op0=ALU.add)
        vm = E("vector", fe, waits=[("v", ma)], inc="v")

        def fs(e, sl=sl):
            return e.activation(out=vtmp[:, sl * 4 + 1:sl * 4 + 2],
                                in_=vtmp[:, sl * 4:sl * 4 + 1], func=AF.Sqrt)
        sm = E("scalar", fs, waits=[("v", vm)], inc="s")

        def fq(e, sl=sl):
            return e.reciprocal(out=vtmp[:, sl * 4 + 2:sl * 4 + 3],
                                in_=vtmp[:, sl * 4 + 1:sl * 4 + 2])
        mq = E("vector", fq, waits=[("s", sm)], inc="v")

        def fn_(e, sl=sl):
            return e.tensor_scalar(out=vtmp[:, sl * 4 + 3:sl * 4 + 4],
                                   in0=stats[:, sl * 2:sl * 2 + 1],
                                   scalar1=vtmp[:, sl * 4 + 2:sl * 4 + 3],
                                   scalar2=-1.0, op0=ALU.mult, op1=ALU.mult)
        vm2 = E("vector", fn_, waits=[("v", mq)], inc="v")
        return vm2, vtmp[:, sl * 4 + 2:sl * 4 + 3], vtmp[:, sl * 4 + 3:sl * 4 + 4]

    def transpose16(src, am, evac_fn, prev_evac=None):
        ev_marks = []
        last_t = None
        for tq in range(4):
            for i in range(4):
                tw = []
                if tq == 0 and i == 0:
                    tw = [("s", am), ("dc", CONST_OK)]
                if tq >= 2 and i == 0:
                    tw.append(("s", ev_marks[tq - 2]))
                if tq < 2 and i == 0 and prev_evac is not None:
                    tw.append(("s", prev_evac[tq - 2]))

                def ft(e, src=src, tq=tq, i=i):
                    return e.transpose(out=pbh[tq % 2][:, i * 128:(i + 1) * 128],
                                       in_=src[:, (tq * 4 + i) * 128:(tq * 4 + i + 1) * 128],
                                       identity=ident)
                last_t = E("tensor", ft, waits=tw, inc="t" if i == 3 else None)
            em = E("scalar", evac_fn(tq), waits=[("t", last_t)], inc="s")
            ev_marks.append(em)
        return last_t, ev_marks

    def ctx_dst(h):
        if h == 0:
            return Ac[:, 0:1024]
        return Aq[:, (h - 1) * 1024: h * 1024]

    ps_s = [pb[0], pb[1], pb[7]]

    def kt_units(kt):
        if kt < 16:
            return [(0, 512), (512, 512)]
        return [(0, 512)]

    def acc_parts(kt):
        if kt < 15:
            return [(0, 512, False), (512, 512, kt == 31)]
        if kt == 15:
            return [(0, 512, False), (512, 512, True)]
        return [(0, 512, kt == 31)]

    units = []
    for h in range(16):
        for kt in range(32):
            for (m0, w) in kt_units(kt):
                units.append((h, kt, m0, w))
    n_u = len(units)
    exp_marks = [None] * n_u
    stt_marks = [None] * n_u
    sum_done = [None] * n_u
    ctxbf_marks = {}
    recip_marks = {}
    binv_marks = {}
    bcast_marks = {}
    pend = {"tensor": [], "scalar": [], "vector": []}

    def flush(eng, upto):
        q = pend[eng]
        while q and q[0][0] <= upto:
            _, fn, wfn, inc, store, key = q.pop(0)
            m0 = E(eng, fn, waits=wfn(), inc=inc)
            if store is not None:
                store[key] = m0

    first_c = [("s", B2_S_DONE), ("t", B2_PE_DONE), ("v", B2_V_DONE),
               ("dc", CONST_OK)]

    def emit_ctx_sum(u):
        h, kt, m0, w = units[u]
        g = h // 4
        esl = u % 8
        wctx = [stt_marks[u]]
        if kt == 0 and h >= 1:
            wctx.append(("v", ctxbf_marks[h - 1]))
        wsum = []
        if kt == 0 and h >= 1:
            wsum.append(("v", recip_marks[h - 1]))
        parts = [(pm0, pw, stp) for (pm0, pw, stp) in acc_parts(kt)
                 if pm0 >= m0 and pm0 < m0 + w]
        last_t = None
        for (pm0, pw, stp) in parts:
            bank = 2 if pm0 < 512 else 3
            boff = pm0 if pm0 < 512 else pm0 - 512

            def fC(e, g=g, kt=kt, pm0=pm0, pw=pw, stp=stp, m0=m0, esl=esl,
                   bank=bank, boff=boff):
                return e.matmul(out=pb[bank][:, boff:boff + pw],
                                lhsT=vsb[:, kt * 512 + g * 128: kt * 512 + (g + 1) * 128],
                                rhs=expm[esl][:, pm0 - m0: pm0 - m0 + pw],
                                start=(kt == 0), stop=stp)
            E("tensor", fC, waits=wctx)
            wctx = []
            sbank = 4 if pm0 < 512 else 5

            def fSum(e, kt=kt, pm0=pm0, pw=pw, stp=stp, m0=m0, esl=esl,
                     sbank=sbank, boff=boff):
                return e.matmul(out=pb[sbank][0:1, boff:boff + pw],
                                lhsT=onesb, rhs=expm[esl][:, pm0 - m0: pm0 - m0 + pw],
                                start=(kt == 0), stop=stp)
            last_t = E("tensor", fSum, waits=wsum, inc="t")
            wsum = []
        sum_done[u] = last_t

    ctxq = []

    def drain_ctxq(u_now):
        while ctxq:
            u2 = ctxq[0]
            if u2 >= u_now:
                break
            h2, kt2, m02, _ = units[u2]
            if kt2 == 0 and m02 == 0 and h2 >= 1 and (h2 - 1) not in ctxbf_marks:
                break
            ctxq.pop(0)
            emit_ctx_sum(u2)

    for u, (h, kt, m0, w) in enumerate(units):
        g = h // 4
        psl = u % 3
        esl = u % 8

        flush("tensor", u)
        flush("scalar", u)
        flush("vector", u)

        wS = list(first_c) if u == 0 else []
        if u >= 3:
            wS.append(("s", exp_marks[u - 3]))

        def fS(e, g=g, kt=kt, h=h, m0=m0, w=w, psl=psl):
            return e.matmul(out=ps_s[psl][:, 0:w],
                            lhsT=kT[:, g * 4096 + kt * 128: g * 4096 + (kt + 1) * 128],
                            rhs=Aq[:, h * 1024 + m0: h * 1024 + m0 + w],
                            start=True, stop=True)
        sm_t = E("tensor", fS, waits=wS, inc="t")

        ctxq.append(u)
        drain_ctxq(u - 1)

        wE = [("t", sm_t)]
        if u >= 6:
            wE.append(stt_marks[u - 6])

        def fE(e, psl=psl, ebl=u % 6, w=w):
            return e.activation(out=expb[ebl][:, 0:w], in_=ps_s[psl][:, 0:w],
                                func=AF.Exp, scale=SCALE)
        em = E("scalar", fE, waits=wE, inc="s")
        exp_marks[u] = em

        if kt in (8, 16, 24) and m0 == 0:
            flush_gamma(2, 212)
        wM = [("s", em)]
        if u >= 8:
            wM.append(("t", sum_done[u - 8]))

        def fM_(e, m0=m0, w=w, kt=kt, esl=esl, ebl=u % 6):
            return e.scalar_tensor_tensor(out=expm[esl][:, 0:w],
                                          in0=qpos[:, m0:m0 + w],
                                          scalar=kpos[:, kt:kt + 1],
                                          in1=expb[ebl][:, 0:w],
                                          op0=ALU.is_ge, op1=ALU.mult)
        vm = ("v", E("vector", fM_, waits=wM, inc="v"))
        stt_marks[u] = vm

        if kt == 31:
            hh_ = h

            def fR(e):
                return e.reciprocal(out=riv, in_=pb[4].rearrange("p f -> p f")[0:1, 0:512])
            def fR0(e):
                return e.reciprocal(out=riv[:, 0:512], in_=pb[4][0:1, 0:512])
            fR0._x = None

            def fR1(e):
                return e.reciprocal(out=riv[:, 512:1024], in_=pb[5][0:1, 0:512])

            def wfR(u=u):
                return [("t", sum_done[u])]
            pend["vector"].append((u + 3, fR0, wfR, "v", None, None))
            pend["vector"].append((u + 3, fR1, wfR, "v", recip_marks, hh_))

            for half in range(2):
                def fB(e, half=half):
                    return e.matmul(out=pb[6][:, 0:512], lhsT=onesf,
                                    rhs=riv[:, half * 512:(half + 1) * 512],
                                    start=True, stop=True)

                def wfB(hh_=hh_, half=half):
                    w2 = [("v", recip_marks[hh_])]
                    if half == 0 and (hh_ - 1, 1) in binv_marks:
                        w2.append(("s", binv_marks[(hh_ - 1, 1)]))
                    if half == 1:
                        w2.append(("s", binv_marks[(hh_, 0)]))
                    return w2
                pend["tensor"].append((u + 4 + half * 2, fB, wfB, "t",
                                       bcast_marks, (hh_, half)))

                def fBe(e, half=half):
                    return e.copy(out=binv[:, half * 512:(half + 1) * 512],
                                  in_=pb[6][:, 0:512])

                def wfBe(hh_=hh_, half=half):
                    return [("t", bcast_marks[(hh_, half)])]
                pend["scalar"].append((u + 5 + half * 2, fBe, wfBe, "s",
                                       binv_marks, (hh_, half)))

            def fCB0(e, hh_=hh_):
                return e.tensor_tensor(out=ctx_dst(hh_)[:, 0:512],
                                       in0=pb[2][:, 0:512],
                                       in1=binv[:, 0:512], op=ALU.mult)

            def wfCB0(hh_=hh_):
                return [("s", binv_marks[(hh_, 0)])]
            pend["vector"].append((u + 6, fCB0, wfCB0, "v", None, None))

            def fCB1(e, hh_=hh_):
                return e.tensor_tensor(out=ctx_dst(hh_)[:, 512:1024],
                                       in0=pb[3][:, 0:512],
                                       in1=binv[:, 512:1024], op=ALU.mult)

            def wfCB1(hh_=hh_):
                return [("s", binv_marks[(hh_, 1)])]
            pend["vector"].append((u + 8, fCB1, wfCB1, "v", ctxbf_marks, hh_))
            flush_gamma(5, 212)

    for d in range(n_u, n_u + 12):
        flush("tensor", d)
        flush("scalar", d)
        flush("vector", d)
        drain_ctxq(d + 1)
    C_V = M("v")
    C_S = M("s")
    E("gpsimd", cast_wd(0), waits=[("t", B2_PE_DONE)], inc="dg0x", inc_n=16)
    flush_gamma(10000, 10000)
    for mi in range(3):
        c0, c1 = min(mat_cols[mi]), max(mat_cols[mi]) + 1

        def frm(e, mi=mi, c0=c0, c1=c1):
            return e.reduce_sum(out=gsum[:, mi:mi + 1], in_=redcol[:, c0:c1],
                                axis=AX.X)
        E("vector", frm, waits=[("gsc", M("gsc")), ("gv", M("gv"))], inc="gv")
    E("gpsimd", lambda e: e.dma_start(out=gscr, in_=gsum[:, 0:3]),
      waits=[("gv", M("gv"))], inc="dgb", inc_n=16)
    E("gpsimd", lambda e: e.dma_start(
        out=grow[:, 0:384], in_=gscr.rearrange("p m -> (p m)").unsqueeze(0)),
      waits=[("dgb", 16)], inc="dgb", inc_n=16)
    GB_OK = M("dgb")


    yadd_marks = {}
    d_last_mm = None
    for hh in range(2):
        for mt in range(8):
            didx = hh * 8 + mt
            sl = didx % 2
            dsem = "dx0" if sl == 0 else "dx1"
            w = ([("s", B2_S_DONE), ("t", B2_PE_DONE), ("v", B2_V_DONE)]
                 if didx < 2 else [("v", yadd_marks[didx - 2])])
            w.append((dsem, M(dsem)))

            def fd(e, mt=mt, hh=hh, sl=sl):
                return e.dma_start(out=xst[sl][:, 0:1024],
                                   in_=xo[mt * 128:(mt + 1) * 128,
                                          hh * 1024:(hh + 1) * 1024])
            dm = E("sync", fd, waits=w, inc=dsem, inc_n=16)

            pbase = 0 if sl == 0 else 2
            w = [("dg0x" if hh == 0 else "dg1x", 16), ("v", C_V), ("s", C_S)]
            if didx >= 2:
                w.append(("v", yadd_marks[didx - 2]))
            mmk = None
            for et in range(16):
                lsrc = ctx_dst(et)
                for nch in range(2):
                    def fm(e, mt=mt, et=et, nch=nch, pbase=pbase, lsrc=lsrc):
                        return e.matmul(
                            out=pb[pbase + nch],
                            lhsT=lsrc[:, mt * 128:(mt + 1) * 128],
                            rhs=Aw[:, et * 1024 + nch * 512: et * 1024 + (nch + 1) * 512],
                            start=(et == 0), stop=(et == 15))
                    mmk = E("tensor", fm, waits=(w if et == 0 and nch == 0 else ()),
                            inc="t" if (et == 15 and nch == 1) else None)
            ym = None
            for nch in range(2):
                def fy(e, mt=mt, hh=hh, nch=nch, sl=sl, pbase=pbase):
                    return e.tensor_tensor(
                        out=Ay[:, mt * 2048 + hh * 1024 + nch * 512:
                               mt * 2048 + hh * 1024 + (nch + 1) * 512],
                        in0=xst[sl][:, nch * 512:(nch + 1) * 512],
                        in1=pb[pbase + nch], op=ALU.add)
                ym = E("vector", fy,
                       waits=([("t", mmk), (dsem, dm)] if nch == 0 else ()), inc="v")
            yadd_marks[didx] = ym
            flush_gamma(6, 212 if hh == 0 else 10000)
        if hh == 0:
            d_last_mm = M("t")
            E("gpsimd", cast_wd(1), waits=[("t", d_last_mm)], inc="dg1x", inc_n=16)
    D_T = M("t")
    D_V = M("v")

    ylt_prev_evac = None
    for mt in range(8):
        sl = mt % 2
        xdep = [("v", D_V)] if mt == 0 else [("v", yadd_marks[8 + mt])]
        vm, rstd, nmr = ln_chain(Ay[:, mt * 2048:(mt + 1) * 2048], sl, xdep)

        def fap(e, mt=mt, sl=sl, rstd=rstd, nmr=nmr):
            return e.activation(out=xlnb[sl], in_=Ay[:, mt * 2048:(mt + 1) * 2048],
                                func=AF.Identity, bias=nmr, scale=rstd)
        am = E("scalar", fap, waits=[("v", vm)], inc="s")

        def mk_evac(tq):
            def fev(e, tq=tq, mt=mt):
                return e.copy(
                    out=Aq.rearrange("p (t m) -> p t m", t=16)[:, tq * 4:(tq + 1) * 4,
                                                               mt * 128:(mt + 1) * 128],
                    in_=pbh[tq % 2].rearrange("p (t m) -> p t m", t=8)[:, 0:4, :])
            return fev
        _, ev_marks = transpose16(xlnb[sl], am, mk_evac, ylt_prev_evac)
        ylt_prev_evac = ev_marks

    D2_T = M("t")
    D2_S = M("s")
    D2_V = M("v")

    ginv_last = None
    for mi in range(3):
        def f1(e, mi=mi):
            return e.reduce_sum(out=g1[:, mi:mi + 1],
                                in_=grow[:, 0:384].rearrange("o (p m) -> o m p", m=3)[:, mi, :],
                                axis=AX.X)
        mf1 = E("vector", f1, waits=[("dgb", GB_OK)], inc="v")

        def f2(e, mi=mi):
            return e.tensor_scalar(out=g1[:, 4 + mi:5 + mi], in0=g1[:, mi:mi + 1],
                                   scalar1=1.0 / NW, scalar2=FFN_EPS,
                                   op0=ALU.mult, op1=ALU.add)
        mf2 = E("vector", f2, waits=[("v", mf1)], inc="v")

        def f3(e, mi=mi):
            return e.reciprocal(out=g1[:, 8 + mi:9 + mi], in_=g1[:, 4 + mi:5 + mi])
        vm = E("vector", f3, waits=[("v", mf2)], inc="v")

        def f4(e, mi=mi):
            return e.matmul(out=pb[6][:, 256 + mi:257 + mi], lhsT=onesf,
                            rhs=g1[:, 8 + mi:9 + mi], start=True, stop=True)
        tm = E("tensor", f4, waits=[("v", vm), ("v", D2_V), ("dc", CONST_OK)], inc="t")

        def f5(e, mi=mi):
            return e.copy(out=ginv[:, mi:mi + 1], in_=pb[6][:, 256 + mi:257 + mi])
        ginv_last = E("scalar", f5, waits=[("t", tm)], inc="s")
    GINV_OK = ginv_last
    GF_T = M("t")

    NG = 11
    NSG = 22
    fq = [gst[:, 0:256], gst[:, 256:512], gst[:, 512:768], gst[:, 768:1024]]
    fq_sems = ["df0", "df1", "df2", "df3"]
    q12_marks = {}
    mm12_sub = {}
    w3q_marks = {}
    mm3_last = {}
    yadd_f = {}
    fq_use = [0, 0, 0, 0]
    dh_use = {0: 0, 1: 0}
    q3k = 0

    def w12_arena(sg):
        return [Aw[:, 0:8192], Aw[:, 8192:16384], Ax[:, 2048:10240]][sg % 3]

    fstg = [Ax32[:, 5120:6144], Ax32[:, 6144:7168], Ax32[:, 7168:8192],
            gst[:, 0:1024]]
    fstg_use = [0, 0, 0, 0]
    fq_chunks = [(0, 4), (4, 4), (8, 4), (12, 4)]

    def emit_quant_sub(sg):
        ar = w12_arena(sg)
        ci = 0
        for mi, wt in enumerate((w1t, w2t)):
            for (t0, nt) in fq_chunks:
                sl = (sg * 8 + ci) % 4
                dsem = fq_sems[sl]
                w = []
                if sg <= 2 and ci == 0 and mi == 0:
                    w = [("t", D2_T), ("s", D2_S), ("v", D2_V)]
                if sg >= 3 and ci == 0 and mi == 0:
                    w.append(("t", mm12_sub[sg - 3]))
                qm_prev = q12_marks.get(("c", sg * 8 + ci - 4))
                if qm_prev:
                    w.append(("v", qm_prev))

                def fd(e, wt=wt, t0=t0, nt=nt, sg=sg, sl=sl):
                    return e.dma_start(
                        out=fstg[sl][:, 0:nt * 256].rearrange("p (t f) -> p t f", t=nt),
                        in_=wt[t0 * 128:(t0 + nt) * 128, sg * 256:(sg + 1) * 256]
                        .rearrange("(t p) f -> p t f", p=128))
                w.append((dsem, fstg_use[sl] * 16))
                fstg_use[sl] += 1
                dm16 = fstg_use[sl] * 16
                E("sync", fd, waits=w, inc=dsem, inc_n=16)

                qm = None
                for tl in range(nt):
                    t = t0 + tl
                    ws = [(dsem, dm16)] if tl == 0 else []
                    if sg == 0 and ci == 0 and mi == 0 and tl == 0:
                        ws.append(("s", GINV_OK))
                    if sg == 2 and ci == 0 and mi == 0 and tl == 0:
                        ws.append(("v", D2_V))
                    if sg >= 3 and ci == 0 and mi == 0 and tl == 0:
                        ws.append(("t", mm12_sub[sg - 3]))

                    def f1(e, sl=sl, tl=tl, mi=mi):
                        return e.activation(
                            out=fstg[sl][:, tl * 256:(tl + 1) * 256],
                            in_=fstg[sl][:, tl * 256:(tl + 1) * 256],
                            func=AF.Identity, scale=ginv[:, mi:mi + 1], bias=magicb)
                    s1 = E("scalar", f1, waits=ws, inc="s")

                    def f2(e, sl=sl, tl=tl):
                        return e.tensor_scalar(
                            out=fstg[sl][:, tl * 256:(tl + 1) * 256],
                            in0=fstg[sl][:, tl * 256:(tl + 1) * 256],
                            scalar1=MAGIC, scalar2=1.0,
                            op0=ALU.subtract, op1=ALU.min)
                    m2 = E("vector", f2, waits=[("s", s1)], inc="v")

                    def f3(e, sl=sl, tl=tl, t=t, mi=mi):
                        return e.tensor_scalar(
                            out=ar[:, mi * 4096 + t * 256:
                                   mi * 4096 + (t + 1) * 256],
                            in0=fstg[sl][:, tl * 256:(tl + 1) * 256],
                            scalar1=-1.0, scalar2=None, op0=ALU.max)
                    qm = E("vector", f3, waits=[("v", m2)], inc="v")
                q12_marks[("c", sg * 8 + ci)] = qm
                ci += 1
        q12_marks[sg] = qm

    def emit_quant_w3(gi):
        nonlocal_k = None
        qm = None
        for fi in range(4):
            for hh in range(2):
                k = gi * 8 + fi * 2 + hh
                sl = k % 2
                dsem = "dh0" if sl == 0 else "dh1"
                w = []
                if k == 0:
                    w = [("t", D2_T), ("s", D2_S)]
                qm_prev = q12_marks.get(("w3", k - 2))
                if qm_prev:
                    w.append(("v", qm_prev))
                if gi > 0 and fi == 0 and hh == 0:
                    w.append(("t", mm3_last[gi - 1]))

                def fd(e, gi=gi, fi=fi, hh=hh, sl=sl):
                    return e.dma_start(
                        out=hst[sl],
                        in_=w3t[(gi * 4 + fi) * 128:(gi * 4 + fi + 1) * 128,
                                hh * 1024:(hh + 1) * 1024])
                w.append((dsem, dh_use[sl] * 16))
                dh_use[sl] += 1
                dm16 = dh_use[sl] * 16
                E("sync", fd, waits=w, inc=dsem, inc_n=16)

                ws = [(dsem, dm16)]
                if k == 0:
                    ws.append(("s", GINV_OK))
                if gi > 0 and fi == 0 and hh == 0:
                    ws.append(("t", mm3_last[gi - 1]))

                def f1(e, sl=sl):
                    return e.activation(out=hst[sl], in_=hst[sl], func=AF.Identity,
                                        scale=ginv[:, 2:3], bias=magicb)
                s1 = E("scalar", f1, waits=ws, inc="s")

                def f2(e, sl=sl):
                    return e.tensor_scalar(out=hst[sl], in0=hst[sl],
                                           scalar1=MAGIC, scalar2=1.0,
                                           op0=ALU.subtract, op1=ALU.min)
                m2 = E("vector", f2, waits=[("s", s1)], inc="v")

                def f3(e, sl=sl, fi=fi, hh=hh):
                    return e.tensor_scalar(
                        out=Ac[:, fi * 2048 + hh * 1024: fi * 2048 + (hh + 1) * 1024],
                        in0=hst[sl], scalar1=-1.0, scalar2=None, op0=ALU.max)
                qm = E("vector", f3, waits=[("v", m2)], inc="v")
                q12_marks[("w3", k)] = qm
        w3q_marks[gi] = qm

    emit_quant_sub(0)
    emit_quant_sub(1)
    prev_group_y = None
    h_marks = {}
    for gi in range(NG):
        emit_quant_w3(gi)
        for half in range(2):
            sg = gi * 2 + half
            mm2k = None
            for fl in range(2):
                fi = half * 2 + fl
                pA = 0 if fi % 2 == 0 else 4
                w = [("v", q12_marks[sg])]
                if gi == 0 and fi == 0:
                    w += [("t", GF_T), ("s", D2_S), ("v", D2_V)]
                if fi >= 2:
                    w.append(("v", h_marks[fi - 2]))
                elif gi > 0:
                    w.append(("v", prev_group_y))
                for t in range(16):
                    for mh in range(2):
                        def fm1(e, t=t, mh=mh, fl=fl, sg=sg, pA=pA):
                            ar = w12_arena(sg)
                            return e.matmul(
                                out=pb[pA + mh],
                                lhsT=ar[:, t * 256 + fl * 128:
                                        t * 256 + (fl + 1) * 128],
                                rhs=Aq[:, t * 1024 + mh * 512: t * 1024 + (mh + 1) * 512],
                                start=(t == 0), stop=(t == 15))
                        E("tensor", fm1, waits=(w if t == 0 and mh == 0 else ()))
                for t in range(16):
                    for mh in range(2):
                        def fm2(e, t=t, mh=mh, fl=fl, sg=sg, pA=pA):
                            ar = w12_arena(sg)
                            return e.matmul(
                                out=pb[pA + 2 + mh],
                                lhsT=ar[:, 4096 + t * 256 + fl * 128:
                                        4096 + t * 256 + (fl + 1) * 128],
                                rhs=Aq[:, t * 1024 + mh * 512: t * 1024 + (mh + 1) * 512],
                                start=(t == 0), stop=(t == 15))
                        mm2k = E("tensor", fm2,
                                 inc="t" if (t == 15 and mh == 1) else None)
                w = [("t", mm2k)]
                if fi >= 2:
                    w.append(("v", h_marks[fi - 2]))
                if gi > 0 and fi < 2:
                    w.append(("t", mm3_last[gi - 1]))

                def fsg0(e, pA=pA, sl=fi % 2):
                    return e.activation(out=sg_buf[sl][:, 0:512], in_=pb[pA],
                                        func=AF.Sigmoid)

                def fsg1(e, pA=pA, sl=fi % 2):
                    return e.activation(out=sg_buf[sl][:, 512:1024], in_=pb[pA + 1],
                                        func=AF.Sigmoid)
                E("scalar", fsg0, waits=w, inc="s")
                sgm = E("scalar", fsg1, inc="s")

                def fh0(e, fi=fi, pA=pA, sl=fi % 2):
                    return e.tensor_tensor(out=hgr[:, fi * 1024: fi * 1024 + 512],
                                           in0=sg_buf[sl][:, 0:512], in1=pb[pA + 2],
                                           op=ALU.mult)

                def fh1(e, fi=fi, pA=pA, sl=fi % 2):
                    return e.tensor_tensor(out=hgr[:, fi * 1024 + 512: (fi + 1) * 1024],
                                           in0=sg_buf[sl][:, 512:1024], in1=pb[pA + 3],
                                           op=ALU.mult)
                wv = [("s", sgm)]
                if gi > 0 and fi < 2:
                    wv.append(("t", mm3_last[gi - 1]))
                E("vector", fh0, waits=wv, inc="v")
                hm = E("vector", fh1, inc="v")
                h_marks[fi] = hm
            mm12_sub[sg] = mm2k
            if sg + 2 <= NSG - 1:
                emit_quant_sub(sg + 2)
        H_DONE = M("v")

        for mt in range(8):
            pbase = 4 if mt % 2 == 0 else 0
            mmk = None
            for fi in range(4):
                w = [("v", h_marks[fi])]
                if fi == 0:
                    w.append(("v", h_marks[3]))
                    w.append(("v", w3q_marks[gi]))
                    if mt >= 2:
                        w.append(("v", yadd_f[(gi, mt - 2)]))
                for nch in range(4):
                    def fm(e, mt=mt, fi=fi, nch=nch, pbase=pbase):
                        return e.matmul(
                            out=pb[pbase + nch],
                            lhsT=hgr[:, fi * 1024 + mt * 128: fi * 1024 + (mt + 1) * 128],
                            rhs=Ac[:, fi * 2048 + nch * 512: fi * 2048 + (nch + 1) * 512],
                            start=(fi == 0), stop=(fi == 3))
                    mmk = E("tensor", fm, waits=(w if nch == 0 else ()),
                            inc="t" if (fi == 3 and nch == 3) else None)
            ym = None
            for nch in range(4):
                def fy(e, mt=mt, nch=nch, pbase=pbase):
                    return e.tensor_tensor(
                        out=Ay[:, mt * 2048 + nch * 512: mt * 2048 + (nch + 1) * 512],
                        in0=Ay[:, mt * 2048 + nch * 512: mt * 2048 + (nch + 1) * 512],
                        in1=pb[pbase + nch], op=ALU.add)
                ym = E("vector", fy, waits=([("t", mmk)] if nch == 0 else ()),
                       inc="v")
            yadd_f[(gi, mt)] = ym

            if gi == NG - 1:
                def fo(e, mt=mt):
                    return e.dma_start(out=out[mt * 128:(mt + 1) * 128, :],
                                       in_=Ay[:, mt * 2048:(mt + 1) * 2048])
                E("sync", fo, waits=[("v", ym)], inc="dob", inc_n=16)
        mm3_last[gi] = mmk
        prev_group_y = yadd_f[(gi, 7)]

    E("sync", lambda e: e.nop(), waits=[("dob", M("dob"))])

    with nc.Block() as block:
        def run(eng_obj, name):
            hwm = {}
            for waits, fn, inc in P.stream[name]:
                for (s, v) in waits:
                    if v is None:
                        continue
                    if v > hwm.get(s, 0):
                        eng_obj.wait_ge(P.sems[s], v)
                        hwm[s] = v
                inst = fn(eng_obj)
                if inc is not None:
                    inst.then_inc(P.sems[inc[0]], inc[1])

        @block.sync
        def _(e):
            run(e, "sync")

        @block.gpsimd
        def _(e):
            run(e, "gpsimd")

        @block.vector
        def _(e):
            run(e, "vector")

        @block.scalar
        def _(e):
            run(e, "scalar")

        @block.tensor
        def _(e):
            run(e, "tensor")

    return nc


_NC_CACHE = None


def kernel(L0_attn_x, L0_attn_w_q, L0_attn_w_kv, L0_attn_w_dense,
           L0_ffn_w1, L0_ffn_w2, L0_ffn_w3):
    global _NC_CACHE
    x = np.asarray(L0_attn_x, np.float32)
    wq = np.ascontiguousarray(np.asarray(L0_attn_w_q, np.float32))
    wkv = np.ascontiguousarray(np.asarray(L0_attn_w_kv, np.float32))
    wd = np.ascontiguousarray(np.asarray(L0_attn_w_dense, np.float32))
    w1t = np.ascontiguousarray(np.asarray(L0_ffn_w1, np.float32).T)
    w2t = np.ascontiguousarray(np.asarray(L0_ffn_w2, np.float32).T)
    w3t = np.ascontiguousarray(np.asarray(L0_ffn_w3, np.float32).T)

    ident = np.eye(128, dtype=ml_dtypes.bfloat16)
    onesb = np.ones((128, 1), dtype=ml_dtypes.bfloat16)
    onesf = np.ones((1, 128), dtype=np.float32)
    kposd = (np.arange(128, dtype=np.float32)[:, None]
             + 128.0 * np.arange(32, dtype=np.float32)[None, :])
    magicv = np.full((128, 1), MAGIC, dtype=np.float32)

    if _NC_CACHE is None:
        _NC_CACHE = build_nc()
    nc = _NC_CACHE

    in_maps = []
    xbt_all = [np.ascontiguousarray(x[b].T) for b in range(B)]
    for c in range(8):
        b, j = c // 4, c % 4
        blocks = SLOT_BLOCKS[j]
        xo = np.concatenate([x[b, 256 * bs:256 * bs + 256] for bs in blocks], axis=0)
        qp = np.concatenate([256.0 * bs + np.arange(256, dtype=np.float32)
                             for bs in blocks])
        qposd = np.ascontiguousarray(np.broadcast_to(qp[None, :], (128, NQO)))
        in_maps.append(dict(
            xbt=xbt_all[b], xot=np.ascontiguousarray(xo.T),
            xo=np.ascontiguousarray(xo),
            wq=wq, wkv=wkv, wd=wd, w1t=w1t, w2t=w2t, w3t=w3t,
            qposd=qposd, kposd=kposd, identd=ident, onesbd=onesb, onesfd=onesf,
            magicd=magicv,
        ))

    res = run_bass_kernel_spmd(nc, in_maps, core_ids=list(range(8)))

    y = np.empty((B, SQ, H), np.float32)
    for c in range(8):
        b, j = c // 4, c % 4
        oc = np.asarray(res.results[c]["out"])
        for s, bs in enumerate(SLOT_BLOCKS[j]):
            y[b, 256 * bs:256 * bs + 256] = oc[256 * s:256 * s + 256]
    return y
